# version 14
# baseline (speedup 1.0000x reference)
import numpy as np
import concourse.bass as bass
import concourse.mybir as mybir
from concourse.bass_utils import run_bass_kernel_spmd

F32 = mybir.dt.float32
BF16 = mybir.dt.bfloat16
I8 = mybir.dt.int8
U8 = mybir.dt.uint8
AF = mybir.ActivationFunctionType
ALU = mybir.AluOpType
AX = mybir.AxisListType

D, S, T = 1024, 4096, 2048
NH, DK, IF = 16, 64, 2816
NCD, NTO, NTA, NI = 8, 16, 32, 22
EPS = 1e-5
ENGS = ("sync", "scalar", "vector", "tensor", "gpsimd")
DMAQ = ("sync", "scalar", "gpsimd")


class Sched:
    NCH = 5

    def __init__(self):
        self.q = {e: [] for e in ENGS}
        self.ccnt = {e: 0 for e in ENGS}
        self.rcnt = {(e, ch): 0 for e in DMAQ for ch in range(self.NCH)}
        self.dcnt = {e: 0 for e in DMAQ}

    def op(self, eng, fn, wait=(), inc=0):
        self.q[eng].append(("op", fn, [t for t in wait if t is not None], inc))
        if inc:
            self.ccnt[eng] += inc
            return ("C", eng, self.ccnt[eng])
        return None

    def _dma(self, eng, fn, wait, ch):
        self.q[eng].append(("dma", fn, [t for t in wait if t is not None], ch))
        self.rcnt[(eng, ch)] += 16
        return ("R", eng, self.rcnt[(eng, ch)], ch)

    def dma(self, eng, out, in_, wait=(), ch=0):
        return self._dma(eng, lambda e, o=out, i=in_: e.dma_start(out=o, in_=i), wait, ch)

    def dmat(self, eng, out, in_, wait=(), ch=0):
        return self._dma(eng, lambda e, o=out, i=in_: e.dma_start_transpose(out=o, in_=i),
                         wait, ch)

    def confirm(self, eng, wait=(), at=None):
        if at is None:
            targets = [(ch, self.rcnt[(eng, ch)]) for ch in range(self.NCH)
                       if self.rcnt[(eng, ch)]]
        else:
            assert at[0] == "R" and at[1] == eng
            targets = [(at[3], at[2])]
        self.q[eng].append(("confirm", targets, [t for t in wait if t is not None], 1))
        self.dcnt[eng] += 1
        return ("D", eng, self.dcnt[eng])

    def render_body(self, eng_name, sems):
        lw = {}

        def wait_tok(eng_obj, tok):
            kind, e2, c2 = tok[0], tok[1], tok[2]
            if kind == "R" and e2 != eng_name:
                raise AssertionError(
                    f"unsafe cross-engine wait on raw DMA sem {e2} (count {c2}) from "
                    f"{eng_name} at queue idx {getattr(self, '_dbg_idx', None)}")
            key = (kind, e2, tok[3]) if kind == "R" else (kind, e2)
            sem = sems[key]
            if c2 > lw.get(key, 0):
                eng_obj.wait_ge(sem, c2)
                lw[key] = c2

        def emit(eng_obj):
            for idx, (kind, fn, waits, inc) in enumerate(self.q[eng_name]):
                self._dbg_idx = (eng_name, idx)
                if kind == "confirm":
                    for ch, tgt in fn:
                        wait_tok(eng_obj, ("R", eng_name, tgt, ch))
                    for t in waits:
                        wait_tok(eng_obj, t)
                    eng_obj.nop().then_inc(sems[("D", eng_name)], 1)
                    continue
                for t in waits:
                    wait_tok(eng_obj, t)
                ins = fn(eng_obj)
                if kind == "dma":
                    ins.then_inc(sems[("R", eng_name, inc)], 16)
                elif inc:
                    ins.then_inc(sems[("C", eng_name)], inc)
        return emit


def build_program(dbg=(), stage="full"):
    dbg = set(dbg)
    nc = bass.Bass("TRN2", target_bir_lowering=False, detect_race_conditions=False)
    sc = Sched()

    def din(name, shape, dt=F32):
        return nc.dram_tensor(name, shape, dt, kind="ExternalInput").ap()

    x_in = din("x_full", [S, D])
    wq_in = din("w_q", [D, D])
    wk_in = din("w_k", [DK, D])
    wv_in = din("w_v", [DK, D])
    wo_in = din("w_out", [D, D])
    wg_in = din("w_gate", [IF, D])
    wu_in = din("w_up", [IF, D])
    wd_in = din("w_down", [D, IF])
    wmod_in = din("wmod", [D, 6 * D])
    c_in = din("c_b", [1, D])
    ss_in = din("softmax_scale", [1, 1])
    cx_in = din("clip_x", [1, 1])
    cd_in = din("clip_down", [1, 1])
    id_in = din("identity", [128, 128])
    ones_in = din("ones_row", [1, 128])
    y_out = nc.dram_tensor("y", [T, D], F32, kind="ExternalOutput").ap()

    def scr(name, shape, dt):
        return nc.dram_tensor(name, shape, dt, kind="Internal").ap()

    wq_bd = scr("wq_bd", [D, D], BF16)
    wk_bd = scr("wk_bd", [DK, D], BF16)
    wv_bd = scr("wv_bd", [DK, D], BF16)
    wo_bd = scr("wo_bd", [D, D], BF16)
    ng_bd = scr("ng_bd", [IF, D], BF16)
    nu_bd = scr("nu_bd", [IF, D], BF16)
    nd_bd = scr("nd_bd", [D, IF], BF16)
    h1_bd = scr("h1_bd", [S, D], BF16)
    k_bd = scr("k_bd", [DK, S], BF16)
    v_bd = scr("v_bd", [DK, S], BF16)
    attn_bd = scr("attn_bd", [NH, DK, T], BF16)
    htm_bd = scr("htm_bd", [T, D], BF16)
    n1_bd = scr("n1_bd", [T, D], BF16)
    h2_bd = scr("h2_bd", [NI, 128, T], BF16)
    x1_bd = scr("x1_bd", [T, D], F32)
    mrow_bd = scr("mrow_bd", [1, 2 * D], BF16)
    brow_bd = scr("brow_bd", [1, D + 2 * DK], BF16)
    sd_bd = scr("sd_bd", [1, D], F32)
    cc1_i = scr("cc1_i", [1, 128], F32)
    cc1_o = scr("cc1_o", [1, 128], F32)
    cc2_i = scr("cc2_i", [1, 128], F32)
    cc2_o = scr("cc2_o", [1, 128], F32)

    A = nc.alloc_sbuf_tensor

    def region(name, nbytes):
        h = A(name, [128, nbytes], U8, align_bytes=64)
        return nc.lookup_mloc(h).addr

    def at(name, base, off, shape, dt):
        return nc.alloc_sbuf_tensor_at(name, shape, dt, offset=base + off, align_bytes=32)

    RA = region("RA", 16384)
    RB = region("RB", 12544)
    RC = region("RC", 28672)
    RD = region("RD", 32768)
    RE = region("RE", 16384)
    RF = region("RF", 32768)
    RG = region("RG", 8448)
    RH = region("RH", 4096)
    RW = region("RW", 16384)

    x_stream = [at(f"xs_{j}", RA, j * 4096, [128, D], F32) for j in range(2)]
    xt4 = [at(f"xt_{j}", RA, j * 4096, [128, D], F32) for j in range(4)]
    y_st = [at(f"yst_{j}", RA, j * 4096, [128, D], F32) for j in range(4)]

    wmod_sb = at("wmod_sb", RB, 0, [128, 6 * D], BF16)
    wmod_sbB = at("wmod_sbB", RF, 0, [128, 6 * D], BF16)
    k2 = at("k2", RB, 0, [128, S], BF16)
    v_tok = at("v_tok", RB, 8192, [128, NTA * DK], BF16)
    h2s = [at(f"h2s_{j}", RB, j * 1024, [128, 512], BF16) for j in range(2)]
    n2i8 = [at(f"n2i8_{j}", RB, 2048 + j * 512, [128, 512], I8) for j in range(2)]
    n2bf = [at(f"n2bf_{j}", RB, 4096 + j * 1024, [128, 512], BF16) for j in range(2)]

    mod_row = at("mod_row", RC, 0, [1, 6 * D], BF16)
    srow_g = at("srow_g", RC, 12288, [1, 2 * D], BF16)
    brow = at("brow", RC, 16384, [1, D + 2 * DK], BF16)
    wn_ld = [at(f"wnld_{j}", RC, j * 11264, [128, IF], F32) for j in range(2)]
    wn_i8 = [at(f"wni8_{j}", RC, 22528 + j * 2816, [128, IF], I8) for j in range(2)]
    dstr = [at(f"dstr_{j}", RC, 8192 + j * 2048, [128, D], BF16) for j in range(2)]

    h1f = [at(f"h1f_{c}", RD, c * 4096, [128, T], BF16) for c in range(NCD)]
    n1_fm = [at(f"n1f_{c}", RD, c * 4096, [128, T], BF16) for c in range(NCD)]

    w_qT = at("w_qT", RE, 0, [128, NCD * D], BF16)
    attn_lhs = [at(f"alhs_{c}", RE, c * 1024, [128, 512], BF16) for c in range(NCD)]
    silu_sb = [at(f"silu_{j}", RE, j * 4096, [128, T], BF16) for j in range(2)]
    h2_st = [at(f"h2st_{j}", RE, 8192 + j * 4096, [128, T], BF16) for j in range(2)]

    q_fm = [at(f"qfm_{c}", RF, c * 4096, [128, T], BF16) for c in range(NCD)]
    dstr_all = [at(f"dsta_{i}", RF if i < 16 else RE,
                   (i * 2048 if i < 16 else (i - 16) * 2048), [128, D], BF16)
                for i in range(NI)]
    gres = [at(f"gres_{c}", RF, c * 2816, [128, 1408], BF16) for c in range(NCD)]
    ures = [at(f"ures_{c}",
               RW if c < 5 else (RG if c < 7 else RH),
               (c * 2816 if c < 5 else ((c - 5) * 2816 if c < 7 else 0)),
               [128, 1408], BF16) for c in range(NCD)]

    v1 = at("v1", RG, 0, [128, NTA * 65], BF16)
    P_sb = [at(f"P_{j}", RG, 4224 + j * 2048, [128, 1024], BF16) for j in range(2)]

    w_kT = at("w_kT", RH, 0, [128, NCD * DK], BF16)
    w_vT = at("w_vT", RH, 1024, [128, NCD * DK], BF16)
    w_kvT = at("w_kvT", RH, 2048, [128, NCD * 128], BF16)
    w_oT = at("w_oT", RW, 0, [128, NCD * D], BF16)

    ident = A("ident", [128, 128], F32)
    onesb = A("onesb", [1, 128], BF16)
    onesf = A("onesf", [1, 128], F32)
    onesf_col = A("onesf_col", [128, 1], F32)
    eps_col = A("eps_col", [128, 1], F32)
    scal = A("scal", [1, 16], F32)
    SS, SIG1, SIG2, TMPA, GMAX1, S1, INV1, GMAX2, S2, INV2, MAXK2, TMPB, WQC = range(13)
    c_row = A("c_row", [8, 128], F32)
    c_row2 = A("c_row2", [8, 128], F32)
    c_cols = A("c_cols", [128, 8], BF16)
    mod_cols = A("mod_cols", [128, 16], F32)
    mod_cols_bf = A("mod_cols_bf", [128, 16], BF16)
    bias_cols = A("bias_cols", [128, 9], F32)
    bq_ss = A("bq_ss", [128, 8], F32)
    ca1ss = A("ca1ss", [128, 8], F32)
    sq_col = A("sq_col", [128, 1], F32)
    fold_t = A("fold_t", [16, 128], BF16)
    fold_f = A("fold_f", [16, 128], F32)
    gb_row = A("gb_row", [1, D], F32)
    csc_row = A("csc_row", [1, D], F32)
    cb1_b = A("cb1_b", [128, D], BF16)
    sb_b = A("sb_b", [128, D], BF16)
    ga_b = A("ga_b", [128, D], BF16)
    colsc_b = A("colsc_b", [128, D], BF16)
    bq2_row = A("bq2_row", [1, 8], F32)
    negB_row = A("negB_row", [1, 8], F32)
    negB_cols = A("negB_cols", [128, 8], F32)
    q_sq = A("q_sq", [128, D], F32)
    kv_stage = [A(f"kvs_{j}", [128, 512], BF16) for j in range(2)]
    rd_bf = A("rd_bf", [1, 512], BF16)
    ssrow_bf = A("ssrow_bf", [1, 1], BF16)
    tpv = [A(f"tpv_{j}", [64, 512], BF16) for j in range(2)]
    attn_stage = [A(f"astg_{j}", [64, 512], BF16) for j in range(3)]
    h1t = [A(f"h1t_{j}", [128, D], BF16) for j in range(4)]
    htm_res = [at(f"htm_{j}", RD, j * 2048, [128, D], BF16) for j in range(NTO)]
    stats = [A(f"stats_{j}", [128, 2], F32) for j in range(4)]
    bn6 = [A(f"bn6_{j}", [128, 12], F32) for j in range(4)]
    rstd = [A(f"rstd_{j}", [128, 1], F32) for j in range(4)]
    hmax_acc = A("hmax_acc", [128, NTO], F32)
    h2max_acc = A("h2max_acc", [128, NI], F32)
    fold_col = A("fold_col", [128, 1], F32)
    cc_row = A("cc_row", [1, 128], F32)
    inv1_col = A("inv1_col", [128, 1], F32)
    wmax = A("wmax", [128, 2 * NI + NCD], F32)
    winv = A("winv", [128, 1], F32)
    sg_cols = A("sg_cols", [128, NI], F32)
    su_cols = A("su_cols", [128, NI], F32)
    n1_i8 = [A(f"n1i8_{j}", [128, D], I8) for j in range(2)]

    _pguards = {}
    _porder = {}

    def palloc(name, shape, dt=F32):
        g = nc.psum_tensor(name, shape, dt)
        h = g.__enter__()
        _pguards[name] = g
        _porder[name] = len(_porder)
        return h

    def pfree(*names):
        for n in sorted(names, key=lambda n: -_porder[n]):
            _pguards.pop(n).__exit__(None, None, None)

    def dump(tag, ap, shape, dt=F32, wait=()):
        if tag not in dbg:
            return None
        o = nc.dram_tensor(f"dbg_{tag}", shape, dt, kind="ExternalOutput").ap()
        return sc.dma("gpsimd", o, ap, wait=wait)

    def finalize(deps):
        cs = sc.confirm("sync")
        cg = sc.confirm("gpsimd")
        safe = [t for t in deps if t is not None and t[0] != "R"] + [cs, cg]
        tm = sc.op("vector", lambda e: e.memset(y_st[0][:], 0.0), safe, 1)
        for j in range(NTO):
            sc.dma("sync", y_out[j * 128:(j + 1) * 128, :], y_st[0][:], [tm])
        sc.confirm("sync")
        return nc, sc

    t_ones = sc.dma("gpsimd", onesb[:], ones_in)
    t_wqc = sc.dma("gpsimd", wq_bd, wq_in)
    t_wkc = sc.dma("gpsimd", wk_bd, wk_in)
    t_wvc = sc.dma("gpsimd", wv_bd, wv_in)
    t_woc = sc.dma("gpsimd", wo_bd, wo_in)
    c_wcast = sc.confirm("gpsimd")
    c_ones = c_wcast
    t_id = sc.dma("sync", ident[:], id_in)
    t_cr = sc.dma("sync", c_row[:], c_in.rearrange("o (p a) -> (o p) a", p=8))
    t_ss = sc.dma("sync", scal[0:1, SS:SS + 1], ss_in)
    t_cx = sc.dma("sync", scal[0:1, TMPA:TMPA + 1], cx_in)
    c_cx = sc.confirm("sync")
    t_sig1 = sc.op("scalar", lambda e: e.activation(out=scal[0:1, SIG1:SIG1 + 1],
                                                    in_=scal[0:1, TMPA:TMPA + 1],
                                                    func=AF.Sigmoid), [c_cx], 1)
    t_cd = sc.dma("sync", scal[0:1, TMPA:TMPA + 1], cd_in, [t_sig1])
    c_cd = sc.confirm("sync")
    t_sig2 = sc.op("scalar", lambda e: e.activation(out=scal[0:1, SIG2:SIG2 + 1],
                                                    in_=scal[0:1, TMPA:TMPA + 1],
                                                    func=AF.Sigmoid), [c_cd], 1)
    t_c1 = sc.op("vector", lambda e: e.memset(onesf[:], 1.0), (), 1)
    t_c2 = sc.op("vector", lambda e: e.memset(onesf_col[:], 1.0), (), 1)
    t_c3 = sc.op("vector", lambda e: e.memset(eps_col[:], EPS), (), 1)
    t_c5 = sc.op("vector", lambda e: e.memset(cc_row[:], 0.0), (), 1)

    t_csil = sc.op("scalar", lambda e: e.activation(out=c_row2[:], in_=c_row[:],
                                                    func=AF.Silu), [c_cx], 1)
    pTc = palloc("pTc", [128, 8])
    t_cct = sc.op("tensor", lambda e: e.transpose(out=pTc[:], in_=c_row2[:],
                                                  identity=ident[0:8, 0:8]), [t_csil, c_cx], 1)
    t_ccol = sc.op("vector", lambda e: e.tensor_copy(out=c_cols[:], in_=pTc[:]),
                   [t_cct], 1)
    pfree("pTc")

    if stage == "m_a":
        return finalize([t_ccol, t_sig2, t_c5, t_c1, t_c2, t_c3, t_ones])

    pm0 = palloc("pm0", [1, 2048])
    pm1 = palloc("pm1", [1, 2048])
    pmt = {0: pm0, 1: pm1, 2: pm0}
    t_modev = []
    wbufs = [wmod_sb, wmod_sbB]
    t_wmod_war = [None, None]
    t_wload = {}

    def issue_wmod(k):
        if k in t_wload or k >= 16:
            return
        a = k % NCD
        if k < 8:
            t_wload[k] = sc.dma("gpsimd", wbufs[k % 2][:, 0:4096],
                                wmod_in[a * 128:(a + 1) * 128, 0:4096],
                                [t_wmod_war[k % 2]], ch=1 + k % 2)
        else:
            t_wload[k] = sc.dma("gpsimd", wbufs[k % 2][:, 0:2048],
                                wmod_in[a * 128:(a + 1) * 128, 4096:6144],
                                [t_wmod_war[k % 2]], ch=1 + k % 2)

    for pnum, groups in enumerate([(0, 1), (2,)]):
        glast = {}
        gl_idx = groups[-1]
        for a in range(NCD):
            k = pnum * NCD + a
            wb = wbufs[k % 2]
            issue_wmod(k)
            issue_wmod(k + 1)
            w8 = sc.confirm("gpsimd", at=t_wload[k])
            extra = t_modev[:2] if (pnum == 1 and a == 0) else []
            for g in groups:
                goff = g * 2048 - (4096 if pnum == 1 else 0)
                for ns in range(4):
                    glast[g] = sc.op("tensor", lambda e, a=a, ns=ns, wb=wb, g=g,
                                     goff=goff: e.matmul(
                        out=pmt[g][0:1, ns * 512:(ns + 1) * 512],
                        lhsT=c_cols[:, a:a + 1],
                        rhs=wb[:, goff + ns * 512:goff + (ns + 1) * 512],
                        start=(a == 0), stop=(a == NCD - 1)),
                        [w8, t_ccol] + extra,
                        1 if (ns == 3 and (g == gl_idx or a == NCD - 1)) else 0)
            t_wmod_war[k % 2] = glast[gl_idx]
        for g in groups:
            t_modev.append(sc.op("vector", lambda e, g=g: e.tensor_copy(
                out=mod_row[0:1, g * 2048:(g + 1) * 2048],
                in_=pmt[g][0:1, :]), [glast[g]], 1))
    pfree("pm0", "pm1")
    t_mod_all = list(t_modev)

    if stage == "m_b":
        return finalize(t_mod_all)

    t_ca1 = sc.op("vector", lambda e: e.tensor_scalar(
        out=mod_row[0:1, D:2 * D], in0=mod_row[0:1, D:2 * D], scalar1=1.0, scalar2=None,
        op0=ALU.add), t_mod_all, 1)
    t_cb1 = sc.op("vector", lambda e: e.tensor_scalar(
        out=mod_row[0:1, 4 * D:5 * D], in0=mod_row[0:1, 4 * D:5 * D], scalar1=1.0,
        scalar2=None, op0=ALU.add), t_mod_all, 1)
    t_g = {}
    for name, off in (("ga", 2 * D), ("gb", 5 * D)):
        dep = [t_g["ga"]] if name == "gb" else []
        t1 = sc.op("vector", lambda e, off=off: e.tensor_scalar(
            out=srow_g[0:1, 0:D], in0=mod_row[0:1, off:off + D], scalar1=0.0, scalar2=2.0,
            op0=ALU.is_ge, op1=ALU.mult), t_mod_all + dep, 1)
        t1b = sc.op("vector", lambda e: e.tensor_scalar(
            out=srow_g[0:1, 0:D], in0=srow_g[0:1, 0:D], scalar1=1.0, scalar2=None,
            op0=ALU.subtract), [t1], 1)
        t2 = sc.op("scalar", lambda e, off=off: e.activation(
            out=srow_g[0:1, D:2 * D], in_=mod_row[0:1, off:off + D], func=AF.Abs),
            t_mod_all + dep + [t1], 1)
        t3 = sc.op("vector", lambda e: e.tensor_scalar(
            out=srow_g[0:1, D:2 * D], in0=srow_g[0:1, D:2 * D], scalar1=0.25, scalar2=None,
            op0=ALU.max), [t2], 1)
        t_g[name] = sc.op("vector", lambda e, off=off: e.tensor_mul(
            out=mod_row[0:1, off:off + D], in0=srow_g[0:1, 0:D], in1=srow_g[0:1, D:2 * D]),
            [t1b, t3], 1)
    t_ga, t_gb = t_g["ga"], t_g["gb"]
    t_gbrow = sc.op("vector", lambda e: e.tensor_copy(out=gb_row[:],
                                                      in_=mod_row[0:1, 5 * D:6 * D]),
                    [t_gb], 1)
    t_mod_fin = [t_ca1, t_cb1, t_ga, t_gb] + t_mod_all
    dump("mod_row", mod_row[:], [1, 6 * D], BF16, t_mod_fin + [t_gbrow])

    if stage == "m_c":
        return finalize(t_mod_fin + [t_gbrow])

    t_mw = sc.dma("sync", mrow_bd, mod_row[0:1, 0:2 * D], [t_ca1] + t_mod_all)
    t_mr2 = sc.dma("sync", fold_t[:], mrow_bd.rearrange("o (p a) -> (o p) a", p=16), [t_mw])
    c_mr2 = sc.confirm("sync")
    t_foldf = sc.op("vector", lambda e: e.tensor_copy(out=fold_f[:], in_=fold_t[:]),
                    [c_mr2], 1)
    pT16 = palloc("pT16", [128, 16])
    t_mt = sc.op("tensor", lambda e: e.transpose(out=pT16[:], in_=fold_f[:],
                                                 identity=ident[0:16, 0:16]),
                 [t_foldf, t_ccol] + t_mod_all, 1)
    t_mcols = sc.op("vector", lambda e: e.tensor_copy(out=mod_cols[:], in_=pT16[:]),
                    [t_mt], 1)
    t_mcolsbf = sc.op("vector", lambda e: e.tensor_copy(out=mod_cols_bf[:], in_=mod_cols[:]),
                      [t_mcols], 1)
    pfree("pT16")

    t_ssb = sc.op("vector", lambda e: e.tensor_copy(out=ssrow_bf[:], in_=scal[0:1, SS:SS + 1]),
                  [c_cx], 1)
    pss = palloc("pss", [128, 1])
    t_ssmm = sc.op("tensor", lambda e: e.matmul(out=pss[:], lhsT=onesb[:],
                                                rhs=ssrow_bf[:], start=True, stop=True),
                   [t_ssb, c_ones] + t_mod_all, 1)
    t_sscol = sc.op("vector", lambda e: e.tensor_copy(out=sq_col[:], in_=pss[:]),
                    [t_ssmm], 1)
    pfree("pss")
    t_ca1ss = sc.op("vector", lambda e: e.tensor_scalar(
        out=ca1ss[:], in0=mod_cols[:, 8:16], scalar1=sq_col[:, 0:1], scalar2=None,
        op0=ALU.mult), [t_mcols, t_sscol], 1)

    bcast_last = [t_sscol]
    pbc_cell = [palloc("pbc", [128, 512])]

    def bcast_row(dst, row_ap, deps, n=D, f32src=False):
        toks = []
        lh = onesf if f32src else onesb
        pt = pbc_cell[0]
        for ns in range((n + 511) // 512):
            w = min(512, n - ns * 512)
            tmm = sc.op("tensor", lambda e, ns=ns, w=w, lh=lh, pt=pt: e.matmul(
                out=pt[:, 0:w], lhsT=lh[:], rhs=row_ap[0:1, ns * 512:ns * 512 + w],
                start=True, stop=True),
                list(deps) + [t_c1, c_ones, t_sscol, bcast_last[0]], 1)
            tev = sc.op("vector", lambda e, dst=dst, ns=ns, w=w, pt=pt: e.tensor_copy(
                out=dst[:, ns * 512:ns * 512 + w], in_=pt[:, 0:w]), [tmm], 1)
            bcast_last[0] = tev
            toks.append(tev)
        return toks

    t_cb1b = bcast_row(cb1_b, mod_row[0:1, 4 * D:5 * D], [t_cb1])
    t_sbb = bcast_row(sb_b, mod_row[0:1, 3 * D:4 * D], t_mod_all)
    t_gab = bcast_row(ga_b, mod_row[0:1, 2 * D:3 * D], [t_ga])

    if stage == "mod":
        return finalize([t_gbrow, t_mcolsbf, t_ca1ss] + t_cb1b + t_sbb + t_gab)

    t_ln, t_h1w, t_xl = [], [], []
    t_xload = {}

    def emit_x_load(j):
        if j in t_xload or j >= 32:
            return
        wl = [t_ln[j - 2]] if j >= 2 else []
        t_xload[j] = sc.dma("sync", x_stream[j % 2][:], x_in[j * 128:(j + 1) * 128, :],
                            wl, ch=1 + j % 2)

    def emit_ln_tile(j):
        buf = x_stream[j % 2]
        r = j % 4
        emit_x_load(j)
        emit_x_load(j + 1)
        tx = sc.confirm("sync", at=t_xload[j])
        t_xl.append(tx)
        wb = [t_ln[j - 4]] if j >= 4 else []
        wh = [t_h1w[j - 4]] if j >= 4 else []
        sc.op("vector", lambda e, b=buf, r=r: e.bn_stats(out=bn6[r][:, 0:6], in_=b[:, 0:512]),
              [tx] + wb, 0)
        tb2 = sc.op("vector", lambda e, b=buf, r=r: e.bn_stats(out=bn6[r][:, 6:12],
                                                               in_=b[:, 512:1024]), [], 1)
        tagg = sc.op("vector", lambda e, r=r: e.bn_aggr(out=stats[r][:], in_=bn6[r][:]),
                     [tb2], 1)
        tsq = sc.op("scalar", lambda e, r=r: e.activation(out=rstd[r][:], in_=stats[r][:, 1:2],
                                                          func=AF.Sqrt, bias=eps_col[:, 0:1],
                                                          scale=1.0), [tagg, t_c3], 1)
        trc = sc.op("vector", lambda e, r=r: e.reciprocal(out=rstd[r][:], in_=rstd[r][:]),
                    [tsq], 1)
        tap = sc.op("vector", lambda e, b=buf, r=r: e.tensor_scalar(
            out=h1t[r][:], in0=b[:], scalar1=stats[r][:, 0:1], scalar2=rstd[r][:, 0:1],
            op0=ALU.subtract, op1=ALU.mult), [trc, tx] + wh, 1)
        t_ln.append(tap)
        sc.dma("gpsimd", h1_bd[j * 128:(j + 1) * 128, :], h1t[r][:], [tap])
        t_h1w.append(sc.confirm("gpsimd"))

    for j in range(2):
        emit_ln_tile(j)

    t_wqT, t_wkTl, t_wvTl, t_woTl = [], [], [], []
    for c in range(NCD):
        t_wqT.append(sc.dmat("sync", w_qT[:, c * D:(c + 1) * D],
                             wq_bd[:, c * 128:(c + 1) * 128], [c_wcast]))
        t_wkTl.append(sc.dmat("sync", w_kT[:, c * DK:(c + 1) * DK],
                              wk_bd[:, c * 128:(c + 1) * 128], [c_wcast]))
        t_wvTl.append(sc.dmat("sync", w_vT[:, c * DK:(c + 1) * DK],
                              wv_bd[:, c * 128:(c + 1) * 128], [c_wcast]))
        t_woTl.append(sc.dmat("sync", w_oT[:, c * D:(c + 1) * D],
                              wo_bd[:, c * 128:(c + 1) * 128], [c_wcast]))

    c_wT = sc.confirm("sync")
    c_woT = c_wT
    dump("w_kT_raw", w_kT[:], [128, NCD * DK], BF16, [c_wT])
    dump("w_vT_raw", w_vT[:], [128, NCD * DK], BF16, [c_wT])
    dump("w_qT_raw", w_qT[:, 0:D], [128, D], BF16, [c_wT])
    dump("wk_bd_d", wk_bd, [DK, D], BF16, [c_wT])
    dump("wv_bd_d", wv_bd, [DK, D], BF16, [c_wT])
    t_bias_mm = []
    pbias = palloc("pbias", [1, D + 2 * DK])
    for half in range(2):
        prev = None
        for c in range(NCD):
            prev = sc.op("tensor", lambda e, c=c, h=half: e.matmul(
                out=pbias[0:1, h * 512:(h + 1) * 512],
                lhsT=mod_cols_bf[:, c:c + 1],
                rhs=w_qT[:, c * D + h * 512:c * D + (h + 1) * 512],
                start=(c == 0), stop=(c == NCD - 1)),
                [t_mcolsbf, c_wT], 1 if c == NCD - 1 else 0)
        t_bias_mm.append(prev)
    for wt, toks, col0 in ((w_kT, t_wkTl, 1024), (w_vT, t_wvTl, 1024 + DK)):
        prev = None
        for c in range(NCD):
            prev = sc.op("tensor", lambda e, c=c, wt=wt, col0=col0: e.matmul(
                out=pbias[0:1, col0:col0 + DK],
                lhsT=mod_cols_bf[:, c:c + 1],
                rhs=wt[:, c * DK:(c + 1) * DK],
                start=(c == 0), stop=(c == NCD - 1)),
                [t_mcolsbf, c_wT], 1 if c == NCD - 1 else 0)
        t_bias_mm.append(prev)
    t_bev = sc.op("vector", lambda e: e.tensor_copy(out=brow[0:1, :],
                                                    in_=pbias[0:1, 0:D + 2 * DK]),
                  t_bias_mm, 1)
    pfree("pbias")
    t_bw = sc.dma("sync", brow_bd, brow[0:1, :], [t_bev])
    t_br = sc.dma("sync", fold_t[0:9, :], brow_bd.rearrange("o (p a) -> (o p) a", p=9),
                  [t_bw, t_foldf])
    c_br = sc.confirm("sync")
    t_foldf2 = sc.op("vector", lambda e: e.tensor_copy(out=fold_f[0:9, :], in_=fold_t[0:9, :]),
                     [c_br, t_mcols], 1)
    pT9 = palloc("pT9", [128, 9])
    t_bt = sc.op("tensor", lambda e: e.transpose(out=pT9[:], in_=fold_f[0:9, :],
                                                 identity=ident[0:9, 0:9]),
                 [t_foldf2, t_mt], 1)
    t_bcols = sc.op("vector", lambda e: e.tensor_copy(out=bias_cols[:], in_=pT9[:]),
                    [t_bt], 1)
    pfree("pT9")
    t_bqss = sc.op("vector", lambda e: e.tensor_scalar(
        out=bq_ss[:], in0=bias_cols[:, 0:8], scalar1=sq_col[:, 0:1], scalar2=None,
        op0=ALU.mult), [t_bcols, t_sscol], 1)

    t_wq_s = []
    for c in range(NCD):
        t_wq_s.append(sc.op("vector", lambda e, c=c: e.tensor_scalar(
            out=w_qT[:, c * D:(c + 1) * D], in0=w_qT[:, c * D:(c + 1) * D],
            scalar1=ca1ss[:, c:c + 1], scalar2=None, op0=ALU.mult),
            [t_ca1ss, c_wT] + t_bias_mm[:2], 1))
    t_kv_p = []
    for c in range(NCD):
        sc.op("vector", lambda e, c=c: e.tensor_scalar(
            out=w_kvT[:, c * 128:c * 128 + DK], in0=w_kT[:, c * DK:(c + 1) * DK],
            scalar1=mod_cols[:, 8 + c:9 + c], scalar2=None, op0=ALU.mult),
            [c_wT, t_mcols] + t_bias_mm[2:], 0)
        t_kv_p.append(sc.op("vector", lambda e, c=c: e.tensor_scalar(
            out=w_kvT[:, c * 128 + DK:(c + 1) * 128], in0=w_vT[:, c * DK:(c + 1) * DK],
            scalar1=mod_cols[:, 8 + c:9 + c], scalar2=None, op0=ALU.mult),
            [c_wT], 1))
    t_wo_f = []
    for c in range(NCD):
        t_wo_f.append(sc.op("vector", lambda e, c=c: e.tensor_mul(
            out=w_oT[:, c * D:(c + 1) * D], in0=w_oT[:, c * D:(c + 1) * D], in1=ga_b[:]),
            [c_woT] + t_gab, 1))

    dump("w_kT_late", w_kT[:], [128, NCD * DK], BF16, t_kv_p + t_bias_mm)
    dump("mod_cols", mod_cols[:], [128, 16], F32, [t_mcols])
    dump("brow", brow[:], [1, D + 2 * DK], BF16, [t_bev])
    dump("bias_cols", bias_cols[:], [128, 9], F32, [t_bcols])
    dump("w_qT0", w_qT[:, 0:D], [128, D], BF16, [t_wq_s[0]])
    dump("w_kvT0", w_kvT[:, 0:128], [128, 128], BF16, [t_kv_p[0]])

    if stage == "wprep":
        return finalize([t_bqss] + t_wq_s + t_kv_p + t_wo_f)

    c_bw = sc.confirm("sync")
    mod_region_free = t_cb1b + t_sbb + t_gab + [t_gbrow, t_bcols, c_bw]
    qs_state = {"ld": [None, None], "i8": [None, None]}
    t_ng_w, t_nu_w, t_nd_w = [], [], []

    def quant_chunk_emit(src, n_bd, i, m0, ncols, out_list, first):
        buf = (m0 + i) % 2
        sc.dma("gpsimd", wn_ld[buf][:, 0:ncols], src[i * 128:(i + 1) * 128, :],
               [qs_state["ld"][buf]] + (mod_region_free if first else []))
        tl = sc.confirm("gpsimd")
        tm = sc.op("vector", lambda e, buf=buf, i=i, m0=m0, nc_=ncols: e.reduce_max(
            out=wmax[:, m0 + i:m0 + i + 1], in_=wn_ld[buf][:, 0:nc_], axis=AX.X,
            apply_absolute_value=True), [tl], 1)
        tiv = sc.op("vector", lambda e, i=i, m0=m0: e.reciprocal(
            out=winv[:], in_=wmax[:, m0 + i:m0 + i + 1]), [tm], 1)
        ti8 = sc.op("vector", lambda e, buf=buf, nc_=ncols: e.tensor_scalar(
            out=wn_i8[buf][:, 0:nc_], in0=wn_ld[buf][:, 0:nc_], scalar1=winv[:, 0:1],
            scalar2=127.0, op0=ALU.mult, op1=ALU.mult), [tiv, qs_state["i8"][buf]], 1)
        qs_state["ld"][buf] = ti8
        sc.dma("gpsimd", n_bd[i * 128:(i + 1) * 128, :], wn_i8[buf][:, 0:ncols], [ti8])
        tw = sc.confirm("gpsimd")
        qs_state["i8"][buf] = tw
        out_list.append(tw)

    quant_steps = []
    for i in range(NI):
        quant_steps.append(lambda i=i: quant_chunk_emit(wg_in, ng_bd, i, 0, D, t_ng_w, i < 2))
    for i in range(NI):
        quant_steps.append(lambda i=i: quant_chunk_emit(wu_in, nu_bd, i, NI, D, t_nu_w, False))
    for i in range(NCD):
        quant_steps.append(lambda i=i: quant_chunk_emit(wd_in, nd_bd, i, 2 * NI, IF,
                                                        t_nd_w, False))
    quant_steps.reverse()

    def quant_tick(n=1):
        for _ in range(n):
            if quant_steps:
                quant_steps.pop()()

    for j in range(2, 16):
        emit_ln_tile(j)


    emit_x_load(16)
    emit_x_load(17)
    t_h1f = [None] * NCD
    for c in range(NCD):
        t_h1f[c] = sc.dmat("sync", h1f[c][:], h1_bd[0:T, c * 128:(c + 1) * 128],
                           [t_h1w[15]])
    c_h1f0 = sc.confirm("sync")
    dump("h1f00", h1f[0][:, 0:512], [128, 512], BF16, [c_h1f0])
    for j in range(16, 32):
        emit_ln_tile(j)

    if stage == "ln":
        return finalize(t_h1f + [t_h1w[-1]])

    t_kvev, t_kbw, t_vbw = [], [], []
    pfree("pbc")
    pbc_cell[0] = None
    pkv = [palloc(f"pkv{t}", [128, 512]) for t in range(4)]
    pq = [palloc(f"pq{t}", [128, 512]) for t in range(4)]

    def emit_kv_half(half, h1f_toks):
        mml = [None] * 4
        for c in range(NCD):
            for ts in range(4):
                mml[ts] = sc.op("tensor", lambda e, c=c, ts=ts: e.matmul(
                    out=pkv[ts][:],
                    lhsT=w_kvT[:, c * 128:(c + 1) * 128],
                    rhs=h1f[c][:, ts * 512:(ts + 1) * 512],
                    start=(c == 0), stop=(c == NCD - 1)),
                    [h1f_toks[c], t_kv_p[c],
                     (t_kvev[(half - 1) * 4 + ts] if half > 0 else None)],
                    1 if c == NCD - 1 else 0)
        for ts in range(4):
            gts = half * 4 + ts
            stg = kv_stage[gts % 2]
            wstg = [t_vbw[gts - 2]] if gts >= 2 else []
            tev = sc.op("scalar", lambda e, ts=ts, stg=stg: e.activation(
                out=stg[:], in_=pkv[ts][:], func=AF.Identity, bias=bias_cols[:, 8:9],
                scale=1.0), [mml[ts], t_bcols] + wstg, 1)
            t_kvev.append(tev)
            sc.dma("gpsimd", k_bd[:, gts * 512:(gts + 1) * 512], stg[0:64, :], [tev])
            sc.dma("gpsimd", v_bd[:, gts * 512:(gts + 1) * 512], stg[64:128, :], [tev])
            t_kbw.append(sc.confirm("gpsimd"))
            t_vbw.append(t_kbw[-1])
            wm = t_mod_all if gts == 0 else []
            sc.dma("sync", k2[0:64, gts * 512:(gts + 1) * 512],
                   k_bd[:, gts * 512:(gts + 1) * 512], [t_kbw[-1]] + wm)
            sc.dma("sync", k2[64:128, gts * 512:(gts + 1) * 512],
                   k_bd[:, gts * 512:(gts + 1) * 512], [])
            for k4 in range(4):
                kt = gts * 4 + k4
                sc.dmat("sync", v_tok[:, kt * DK:(kt + 1) * DK],
                        v_bd[:, kt * 128:(kt + 1) * 128], wm if k4 == 0 else [])

    emit_kv_half(0, [c_h1f0] * NCD)
    t_qev = []
    for pair in range(NCD):
        mml = [None] * 4
        for c in range(NCD):
            for ts in range(4):
                mml[ts] = sc.op("tensor", lambda e, pair=pair, c=c, ts=ts: e.matmul(
                    out=pq[ts][:],
                    lhsT=w_qT[:, c * D + pair * 128:c * D + (pair + 1) * 128],
                    rhs=h1f[c][:, ts * 512:(ts + 1) * 512],
                    start=(c == 0), stop=(c == NCD - 1)),
                    [c_h1f0, t_wq_s[c], t_bev,
                     (t_qev[(pair - 1) * 4 + ts] if pair > 0 else None)],
                    1 if c == NCD - 1 else 0)
        for ts in range(4):
            t_qev.append(sc.op("scalar", lambda e, pair=pair, ts=ts: e.activation(
                out=q_fm[pair][:, ts * 512:(ts + 1) * 512], in_=pq[ts][:], func=AF.Identity,
                bias=bq_ss[:, pair:pair + 1], scale=1.0),
                [mml[ts], t_bqss], 1))
        quant_tick(2)
    dump("qfm0", q_fm[0][:, 0:512], [128, 512], BF16, [t_qev[3]])

    for c in range(NCD):
        sc.dmat("sync", h1f[c][:], h1_bd[T:2 * T, c * 128:(c + 1) * 128],
                [t_h1w[31], t_qev[-1]])
    c_h1f1 = sc.confirm("sync")
    emit_kv_half(1, [c_h1f1] * NCD)

    c_kv2 = sc.confirm("sync")
    t_c4 = sc.op("vector", lambda e: e.memset(v1[:], 1.0), (), 1)
    t_v1 = []
    for kt in range(NTA):
        t_v1.append(sc.op("vector", lambda e, kt=kt: e.tensor_copy(
            out=v1[:, kt * 65:kt * 65 + DK], in_=v_tok[:, kt * DK:(kt + 1) * DK]),
            [c_kv2, t_c4], 1 if kt % 8 == 7 else 0))
    t_v1 = [t for t in t_v1 if t]
    dump("k2", k2[:, 0:512], [128, 512], BF16, [c_kv2])
    dump("v1", v1[:, 0:650], [128, 650], BF16, t_v1)

    t_qdone = t_qev[-1]
    pfree(*[f"pkv{t}" for t in range(4)], *[f"pq{t}" for t in range(4)])
    pbnd = palloc("pbnd", [1, 512])
    pnegb = palloc("pnegb", [128, 8])
    t_prev = None
    t_maxk = None
    for tsl in range(8):
        tq = sc.op("scalar", lambda e, tsl=tsl: e.activation(
            out=q_sq[0:64, 0:512], in_=k2[0:64, tsl * 512:(tsl + 1) * 512],
            func=AF.Square), [c_kv2, t_prev, t_qdone], 1)
        mm = sc.op("tensor", lambda e: e.matmul(
            out=pbnd[:], lhsT=onesf_col[0:64, :], rhs=q_sq[0:64, 0:512],
            start=True, stop=True), [tq, t_c2, t_qdone, t_prev], 1)
        if tsl == 0:
            t_maxk = sc.op("vector", lambda e: e.reduce_max(
                out=scal[0:1, MAXK2:MAXK2 + 1], in_=pbnd[:], axis=AX.X), [mm], 1)
        else:
            tr_ = sc.op("vector", lambda e: e.reduce_max(
                out=scal[0:1, TMPB:TMPB + 1], in_=pbnd[:], axis=AX.X),
                [mm, t_maxk], 1)
            t_maxk = sc.op("vector", lambda e: e.tensor_tensor(
                out=scal[0:1, MAXK2:MAXK2 + 1], in0=scal[0:1, MAXK2:MAXK2 + 1],
                in1=scal[0:1, TMPB:TMPB + 1], op=ALU.max), [tr_], 1)
        t_prev = t_maxk
    t_bq2 = []
    t_prev_mm = t_maxk
    for pair in range(NCD):
        t_acc = None
        for qtr in range(4):
            tq = sc.op("scalar", lambda e, pair=pair, qtr=qtr: e.activation(
                out=q_sq[:, 0:512], in_=q_fm[pair][:, qtr * 512:(qtr + 1) * 512],
                func=AF.Square), [t_qev[pair * 4 + 3], t_prev_mm], 1)
            mm = sc.op("tensor", lambda e: e.matmul(
                out=pbnd[:], lhsT=onesf_col[:], rhs=q_sq[:, 0:512],
                start=True, stop=True), [tq, t_prev_mm], 1)
            if qtr == 0:
                t_acc = sc.op("vector", lambda e, pair=pair: e.reduce_max(
                    out=bq2_row[0:1, pair:pair + 1], in_=pbnd[:], axis=AX.X),
                    [mm], 1)
            else:
                tr_ = sc.op("vector", lambda e: e.reduce_max(
                    out=scal[0:1, TMPB:TMPB + 1], in_=pbnd[:], axis=AX.X),
                    [mm, t_acc], 1)
                t_acc = sc.op("vector", lambda e, pair=pair: e.tensor_tensor(
                    out=bq2_row[0:1, pair:pair + 1], in0=bq2_row[0:1, pair:pair + 1],
                    in1=scal[0:1, TMPB:TMPB + 1], op=ALU.max), [tr_], 1)
            t_prev_mm = t_acc
        t_bq2.append(t_acc)
    t_b1 = sc.op("vector", lambda e: e.tensor_scalar(
        out=bq2_row[:], in0=bq2_row[:], scalar1=scal[0:1, MAXK2:MAXK2 + 1], scalar2=None,
        op0=ALU.mult), t_bq2 + [t_maxk], 1)
    t_b2 = sc.op("scalar", lambda e: e.activation(out=negB_row[:], in_=bq2_row[:],
                                                  func=AF.Sqrt), [t_b1], 1)
    t_b3 = sc.op("vector", lambda e: e.tensor_scalar(
        out=negB_row[:], in0=negB_row[:], scalar1=-1.0, scalar2=None, op0=ALU.mult),
        [t_b2], 1)
    t_b4 = sc.op("tensor", lambda e: e.matmul(out=pnegb[:], lhsT=onesf[:],
                                              rhs=negB_row[:], start=True, stop=True),
                 [t_b3, t_c1, t_prev_mm], 1)
    t_negB = sc.op("vector", lambda e: e.tensor_copy(out=negB_cols[:], in_=pnegb[:]),
                   [t_b4], 1)
    pfree("pbnd", "pnegb")
    dump("negB", negB_cols[:], [128, 8], F32, [t_negB])

    if stage == "qkv":
        return finalize([t_negB, c_kv2] + t_v1 + [t_qev[-1]])

    qkv_done = [t_negB, c_kv2]
    psc = [palloc("pscA", [128, 1024]), palloc("pscB", [128, 1024])]
    ppv = [palloc("ppv0", [65, 512]), palloc("ppv1", [65, 512])]
    pnorm = palloc("pnorm", [64, 512])
    popj = palloc("popj", [128, 512])
    t_ostg_dma = {}
    stage_confirms = {}
    t_pair_done = {}
    t_x1 = [None] * NTO
    t_x1w = [None] * NTO
    t_hmax = [None] * NTO
    t_opj_last = [None]
    prev_state = {"exp": {}, "pv": {}, "evacs": []}

    def emit_attn_pair(qb, pair):
        q0 = qb * 512
        pv_toks = {}
        t_exp = {}
        pexp, ppv_prev = prev_state["exp"], prev_state["pv"]

        def emit_score_exp(kt):
            pS = psc[kt % 2]
            if kt >= 2:
                w_sc = [t_exp[kt - 2]]
            else:
                w_sc = [pexp.get(30 + kt)] if pexp else list(qkv_done)
            sc.op("tensor", lambda e, kt=kt, pS=pS: e.matmul(
                out=pS[:, 0:512],
                lhsT=k2[0:64, kt * 128:(kt + 1) * 128],
                rhs=q_fm[pair][0:64, q0:q0 + 512],
                start=True, stop=True, tile_position=(0, 0)), w_sc, 0)
            mm1 = sc.op("tensor", lambda e, kt=kt, pS=pS: e.matmul(
                out=pS[:, 512:1024],
                lhsT=k2[64:128, kt * 128:(kt + 1) * 128],
                rhs=q_fm[pair][64:128, q0:q0 + 512],
                start=True, stop=True, tile_position=(64, 0)), [], 1)
            if kt >= 2:
                w_ex = [mm1, pv_toks[kt - 2]]
            else:
                w_ex = [mm1] + ([ppv_prev.get(30 + kt)] if ppv_prev else [t_negB])
            t_exp[kt] = sc.op("scalar", lambda e, kt=kt, pS=pS: e.activation(
                out=P_sb[kt % 2][:], in_=pS[:], func=AF.Exp,
                bias=negB_cols[:, pair:pair + 1], scale=1.0), w_ex, 1)

        def emit_pv(kt):
            w_pv = [t_exp[kt]] + ((prev_state["evacs"] or t_v1) if kt == 0 else [])
            sc.op("tensor", lambda e, kt=kt: e.matmul(
                out=ppv[0][:], lhsT=v1[:, kt * 65:(kt + 1) * 65],
                rhs=P_sb[kt % 2][:, 0:512], start=(kt == 0), stop=(kt == NTA - 1)), w_pv, 0)
            pv_toks[kt] = sc.op("tensor", lambda e, kt=kt: e.matmul(
                out=ppv[1][:], lhsT=v1[:, kt * 65:(kt + 1) * 65],
                rhs=P_sb[kt % 2][:, 512:1024], start=(kt == 0), stop=(kt == NTA - 1)), [], 1)

        emit_score_exp(0)
        for kt in range(1, NTA):
            emit_score_exp(kt)
            emit_pv(kt - 1)
        emit_pv(NTA - 1)
        evacs = []
        t_prev_norm = None
        for hl in range(2):
            h = pair * 2 + hl
            stg = attn_stage[(qb * 16 + h) % 3]
            widx = (qb * 16 + h) - 3
            wstg = []
            if widx >= 0:
                qb_o, p_o = divmod(widx, 16)[0], (widx % 16) // 2
                if (qb_o, p_o) in stage_confirms:
                    wstg = [stage_confirms[(qb_o, p_o)]]
            def _recip(e, hl=hl):
                with nc.allow_low_precision("bf16 softmax denominators (honest-bf16 path)"):
                    return e.reciprocal(out=rd_bf[0:1, :], in_=ppv[hl][64:65, :])
            tr_ = sc.op("vector", _recip, [pv_toks[NTA - 1]], 1)
            tcp = sc.op("vector", lambda e, hl=hl: e.tensor_copy(
                out=tpv[hl][:], in_=ppv[hl][0:64, :]), [tr_], 1)
            tbc = sc.op("tensor", lambda e: e.matmul(
                out=pnorm[:], lhsT=onesb[0:1, 0:64],
                rhs=rd_bf[0:1, :], start=True, stop=True), [tr_, t_prev_norm], 1)
            tml = sc.op("vector", lambda e, hl=hl, stg=stg: e.tensor_mul(
                out=stg[:], in0=tpv[hl][:], in1=pnorm[:]),
                [tcp, tbc] + wstg, 1)
            t_prev_norm = tml
            t_ostg_dma[(qb, h)] = sc.dma("sync", attn_bd[h, :, qb * 512:(qb + 1) * 512],
                                         stg[:], [tml])
            evacs.extend([tr_, tcp])
        t_pair_done[(qb, pair)] = sc.confirm("sync")
        stage_confirms[(qb, pair)] = t_pair_done[(qb, pair)]
        prev_state["exp"] = t_exp
        prev_state["pv"] = pv_toks
        prev_state["evacs"] = evacs + [t_prev_norm]

    c_al_of = {}

    def emit_al_loads(qb):
        al_deps = [t_pair_done[(qb, NCD - 1)]]
        if qb == 0:
            al_deps += [t_qev[-1]]
        for c in range(NCD):
            wprev = [t_opj_last[0]] if t_opj_last[0] is not None else []
            sc.dma("sync", attn_lhs[c][:],
                   attn_bd.rearrange("h d t -> (h d) t")[
                       c * 128:(c + 1) * 128, qb * 512:(qb + 1) * 512],
                   al_deps + wprev)
        c_al_of[qb] = sc.confirm("sync")

    def emit_post_block(qb):
        c_al = c_al_of[qb]
        t_xld = {}

        def issue_x(jj):
            if jj in t_xld or jj >= 4:
                return
            j = qb * 4 + jj
            wxb = [t_ln[-1]] if qb == 0 else []
            t_xld[jj] = sc.dma("gpsimd", xt4[jj % 2][:], x_in[j * 128:(j + 1) * 128, :],
                               wxb + ([t_x1[j - 2]] if j >= 2 else []), ch=1 + jj % 2)

        for jj in range(4):
            j = qb * 4 + jj
            xb = xt4[jj % 2]
            x1b = xt4[2 + jj % 2]
            issue_x(jj)
            issue_x(jj + 1)
            tx = sc.confirm("gpsimd", at=t_xld[jj])
            prev_ev = None
            for half in range(2):
                prev = None
                for c in range(NCD):
                    prev = sc.op("tensor", lambda e, c=c, jj=jj, half=half: e.matmul(
                        out=popj[:],
                        lhsT=attn_lhs[c][:, jj * 128:(jj + 1) * 128],
                        rhs=w_oT[:, c * D + half * 512:c * D + (half + 1) * 512],
                        start=(c == 0), stop=(c == NCD - 1)),
                        [c_al, prev_ev,
                         (t_opj_last[0] if (half == 0 and c == 0) else None)] +
                        (t_wo_f if (qb == 0 and jj == 0) else []),
                        1 if c == NCD - 1 else 0)
                prev_ev = sc.op("vector", lambda e, half=half, xb=xb, x1b=x1b: e.tensor_add(
                    out=x1b[:, half * 512:(half + 1) * 512],
                    in0=xb[:, half * 512:(half + 1) * 512],
                    in1=popj[:]), [prev, tx], 1)
            t_x1[j] = prev_ev
            t_x1w[j] = sc.dma("sync", x1_bd[j * 128:(j + 1) * 128, :], x1b[:], [prev_ev])
            t_opj_last[0] = prev_ev
            r = j % 4
            stg = htm_res[j]
            sc.op("vector", lambda e, b=x1b, r=r: e.bn_stats(out=bn6[r][:, 0:6],
                                                             in_=b[:, 0:512]), [prev_ev], 0)
            tb2 = sc.op("vector", lambda e, b=x1b, r=r: e.bn_stats(out=bn6[r][:, 6:12],
                                                                   in_=b[:, 512:1024]), [], 1)
            tagg = sc.op("vector", lambda e, r=r: e.bn_aggr(out=stats[r][:], in_=bn6[r][:]),
                         [tb2], 1)
            tsq = sc.op("scalar", lambda e, r=r: e.activation(
                out=rstd[r][:], in_=stats[r][:, 1:2], func=AF.Sqrt, bias=eps_col[:, 0:1],
                scale=1.0), [tagg], 1)
            trc = sc.op("vector", lambda e, r=r: e.reciprocal(out=rstd[r][:], in_=rstd[r][:]),
                        [tsq], 1)
            tap = sc.op("vector", lambda e, b=x1b, r=r: e.tensor_scalar(
                out=q_sq[:], in0=b[:], scalar1=stats[r][:, 0:1], scalar2=rstd[r][:, 0:1],
                op0=ALU.subtract, op1=ALU.mult), [trc], 1)
            tm1 = sc.op("vector", lambda e: e.tensor_mul(
                out=q_sq[:], in0=q_sq[:], in1=cb1_b[:]), [tap] + t_cb1b, 1)
            tm2 = sc.op("vector", lambda e, stg=stg: e.tensor_add(
                out=stg[:], in0=q_sq[:], in1=sb_b[:]),
                [tm1, tx, c_h1f1, t_kvev[-1], t_qev[-1]] + t_sbb, 1)
            t_hmax[j] = sc.op("vector", lambda e, stg=stg, j=j: e.reduce_max(
                out=hmax_acc[:, j:j + 1], in_=stg[:], axis=AX.X,
                apply_absolute_value=True), [tm2], 1)

    for qb in range(4):
        for pair in range(NCD):
            emit_attn_pair(qb, pair)
            quant_tick(2)
            if pair == 1 and qb >= 1:
                emit_post_block(qb - 1)
        emit_al_loads(qb)
    emit_post_block(3)
    quant_tick(60)
    dump("attn00", attn_bd[0, :, 0:512], [DK, 512], BF16, [t_pair_done[(0, 0)]])
    dump("attn30", attn_bd[0, :, 1536:2048], [DK, 512], BF16, [t_pair_done[(3, 0)]])
    dump("attn01", attn_bd[1, :, 0:512], [DK, 512], BF16, [t_pair_done[(0, 0)]])
    dump("x1_0", xt4[2][:, 0:D], [128, D], F32, [t_x1[14]])

    if stage == "attn":
        return finalize([t_hmax[-1]] + t_nd_w)

    pfree("pscA", "pscB", "ppv0", "ppv1", "pnorm", "popj")
    pP5a = palloc("pP5a", [8, 128])
    pP5b = palloc("pP5b", [1, 128])
    pP5c = palloc("pP5c", [128, 1])
    pP5d = palloc("pP5d", [128, 1])
    t_sdt = sc.op("tensor", lambda e: e.transpose(
        out=pP5a[:], in_=wmax[:, 2 * NI:2 * NI + NCD],
        identity=ident[:]), t_nd_w + [t_pair_done[(3, 7)], t_x1[NTO - 1]], 1)
    t_sdc = sc.op("vector", lambda e: e.tensor_copy(out=fold_f[0:8, :], in_=pP5a[:]),
                  [t_sdt, t_foldf2], 1)
    sc.dma("gpsimd", sd_bd.rearrange("o (p a) -> (o p) a", p=8), fold_f[0:8, :],
           [t_sdc])
    t_sdw = sc.confirm("gpsimd")

    tf1 = sc.op("vector", lambda e: e.reduce_max(out=fold_col[:], in_=hmax_acc[:],
                                                 axis=AX.X), list(t_hmax), 1)
    tf2 = sc.op("tensor", lambda e: e.transpose(out=pP5b[:],
                                                in_=fold_col[:], identity=ident[:]),
                [tf1, t_sdc], 1)
    tf3 = sc.op("vector", lambda e: e.tensor_copy(out=cc_row[0:1, 0:128], in_=pP5b[:]),
                [tf2, t_c5], 1)
    t_ccw = sc.dma("gpsimd", cc1_i, cc_row[0:1, :], [tf3])
    t_cc1 = sc.op("gpsimd", lambda e: e.collective_compute(
        "AllReduce", ALU.max, replica_groups=[[0, 1, 2, 3, 4, 5, 6, 7]],
        ins=[cc1_i.opt()], outs=[cc1_o.opt()]), [t_ccw], 1)
    sc.dma("gpsimd", cc_row[0:1, :], cc1_o, [t_cc1])
    c_ccr = sc.confirm("gpsimd")
    tg1 = sc.op("vector", lambda e: e.reduce_max(out=scal[0:1, GMAX1:GMAX1 + 1],
                                                 in_=cc_row[0:1, :], axis=AX.X), [c_ccr], 1)
    ts1 = sc.op("vector", lambda e: e.tensor_scalar(
        out=scal[0:1, S1:S1 + 1], in0=scal[0:1, GMAX1:GMAX1 + 1],
        scalar1=scal[0:1, SIG1:SIG1 + 1], scalar2=1.0 / 127.0, op0=ALU.mult, op1=ALU.mult),
        [tg1, t_sig1], 1)
    tiv1 = sc.op("vector", lambda e: e.reciprocal(out=scal[0:1, INV1:INV1 + 1],
                                                  in_=scal[0:1, S1:S1 + 1]), [ts1], 1)
    tivmm = sc.op("tensor", lambda e: e.matmul(out=pP5c[:],
                                               lhsT=onesf[:],
                                               rhs=scal[0:1, INV1:INV1 + 1],
                                               start=True, stop=True), [tiv1, tf3], 1)
    t_inv1c = sc.op("vector", lambda e: e.tensor_copy(out=inv1_col[:], in_=pP5c[:]),
                    [tivmm], 1)
    t_n1w = []
    t_n1wr = {}
    for j in range(NTO):
        buf = j % 2
        wb = [t_n1w[j - 2]] if j >= 2 else []
        ti = sc.op("vector", lambda e, buf=buf, j=j: e.tensor_scalar(
            out=n1_i8[buf][:], in0=htm_res[j][:], scalar1=inv1_col[:, 0:1], scalar2=None,
            op0=ALU.mult), [t_inv1c, t_hmax[j]] + wb, 1)
        t_n1wr[j] = sc.dma("gpsimd", n1_bd[j * 128:(j + 1) * 128, :], n1_i8[buf][:],
                           [ti], ch=1 + j % 2)
        if j >= 1:
            t_n1w.append(sc.confirm("gpsimd", at=t_n1wr[j - 1]))
    t_n1w.append(sc.confirm("gpsimd", at=t_n1wr[NTO - 1]))
    c_n1w = t_n1w[-1]
    c_n1f_h = [None, None]
    for hh2 in range(2):
        for c in range(NCD):
            sc.dmat("sync", n1_fm[c][:, hh2 * 1024:(hh2 + 1) * 1024],
                    n1_bd[hh2 * 1024:(hh2 + 1) * 1024, c * 128:(c + 1) * 128],
                    [t_n1w[max(8 * hh2 + 7, 2 * c + hh2)], t_qev[-1], t_kvev[-1]])
        c_n1f_h[hh2] = sc.confirm("sync")
    c_n1f = c_n1f_h[1]
    t_n1f = [c_n1f] * NCD
    dump("n1f0", n1_fm[0][:, 0:512], [128, 512], BF16, [c_n1f])
    dump("hmaxd", hmax_acc[:, 0:NTO], [128, NTO], F32, t_hmax)

    if stage == "n1":
        return finalize([c_n1f])

    tsgs = sc.op("vector", lambda e: e.tensor_scalar(
        out=scal[0:1, WQC:WQC + 1], in0=scal[0:1, GMAX1:GMAX1 + 1],
        scalar1=scal[0:1, SIG1:SIG1 + 1], scalar2=1.0 / (127.0 * 127.0),
        op0=ALU.mult, op1=ALU.mult), [tg1], 1)
    tsgmm = sc.op("tensor", lambda e: e.matmul(out=pP5d[:],
                                               lhsT=onesf[:], rhs=scal[0:1, WQC:WQC + 1],
                                               start=True, stop=True),
                  [tsgs, tivmm, t_inv1c], 1)
    tsgc = sc.op("vector", lambda e: e.tensor_copy(
        out=fold_col[:], in_=pP5d[:]), [tsgmm, tf1], 1)
    t_sg = sc.op("vector", lambda e: e.tensor_scalar(
        out=sg_cols[:], in0=wmax[:, 0:NI], scalar1=fold_col[:, 0:1], scalar2=None,
        op0=ALU.mult), [tsgc] + t_ng_w, 1)
    t_su = sc.op("vector", lambda e: e.tensor_scalar(
        out=su_cols[:], in0=wmax[:, NI:2 * NI], scalar1=fold_col[:, 0:1], scalar2=None,
        op0=ALU.mult), [tsgc] + t_nu_w, 1)
    pfree("pP5a", "pP5b", "pP5c", "pP5d")
    pgt = [palloc(f"pgt{t}", [128, 512]) for t in range(4)]
    put = [palloc(f"put{t}", [128, 512]) for t in range(4)]

    t_h2w, t_h2m = [], []
    t_silu_p = [None, None]
    t_h2_p = [None, None]
    t_gev = {}
    t_h2mul = {}
    attn_end = [t_x1[NTO - 1], t_hmax[NTO - 1], prev_state["pv"][NTA - 1]]
    c_ghalf = [None, None]

    def emit_gu_half(h):
        lo = h * 11 * 128
        war = attn_end if h == 0 else [t_gev[10], t_h2mul[10]]
        extra = [t_nd_w[-1], t_n1f[-1]] if h == 0 else []
        last = None
        for c in range(NCD):
            sc.dmat("sync", gres[c][:], ng_bd[lo:lo + 1408, c * 128:(c + 1) * 128],
                    [t_ng_w[h * 11 + 10]] + war + extra)
            last = sc.dmat("sync", ures[c][:], nu_bd[lo:lo + 1408, c * 128:(c + 1) * 128],
                           [t_nu_w[h * 11 + 10]] + war)
        c_ghalf[h] = sc.confirm("sync", at=last)

    emit_gu_half(0)
    for i in range(NI):
        buf = i % 2
        hf, io = i // 11, i % 11
        if i == 11:
            emit_gu_half(1)
        c_gu = c_ghalf[hf]
        tg_rd = [c_gu] * NCD
        tu_rd = [c_gu] * NCD
        g_last = u_last = None
        for ts4 in range(4):
            for c in range(NCD):
                wfirst = []
                if c == 0 and ts4 == 0:
                    wfirst = [t_gev.get(i - 1)]
                    if i == 0:
                        wfirst += [tivmm, tsgmm, t_b4]
                g_last = sc.op("tensor", lambda e, c=c, ts4=ts4, io=io: e.matmul(
                    out=pgt[ts4][:], lhsT=gres[c][:, io * 128:(io + 1) * 128],
                    rhs=n1_fm[c][:, ts4 * 512:(ts4 + 1) * 512],
                    start=(c == 0), stop=(c == NCD - 1)),
                    [tg_rd[c], c_n1f_h[ts4 // 2]] + wfirst,
                    1 if (c == NCD - 1 and ts4 == 3) else 0)
        for ts4 in range(4):
            for c in range(NCD):
                wfirst = []
                if c == 0 and ts4 == 0:
                    wfirst = [t_h2_p[buf], t_silu_p[(i - 1) % 2] if i >= 1 else None]
                u_last = sc.op("tensor", lambda e, c=c, ts4=ts4, io=io: e.matmul(
                    out=put[ts4][:], lhsT=ures[c][:, io * 128:(io + 1) * 128],
                    rhs=n1_fm[c][:, ts4 * 512:(ts4 + 1) * 512],
                    start=(c == 0), stop=(c == NCD - 1)),
                    [tu_rd[c]] + wfirst, 1 if (c == NCD - 1 and ts4 == 3) else 0)
        tsil = None
        for ts4 in range(4):
            tsil = sc.op("scalar", lambda e, i=i, buf=buf, ts4=ts4: e.activation(
                out=silu_sb[buf][:, ts4 * 512:(ts4 + 1) * 512], in_=pgt[ts4][:],
                func=AF.Silu, bias=0.0, scale=sg_cols[:, i:i + 1]),
                [g_last, t_sg, t_silu_p[buf]] + ([t_x1[NTO - 1]] if i < 2 else []),
                1 if ts4 == 3 else 0)
        t_gev[i] = tsil
        th2 = None
        for ts4 in range(4):
            th2 = sc.op("vector", lambda e, buf=buf, ts4=ts4: e.tensor_mul(
                out=h2_st[buf][:, ts4 * 512:(ts4 + 1) * 512],
                in0=silu_sb[buf][:, ts4 * 512:(ts4 + 1) * 512], in1=put[ts4][:]),
                [tsil, u_last, t_h2_p[buf]], 1 if ts4 == 3 else 0)
        t_h2mul[i] = th2
        t_silu_p[buf] = th2
        t_h2m.append(sc.op("vector", lambda e, i=i, buf=buf: e.reduce_max(
            out=h2max_acc[:, i:i + 1], in_=h2_st[buf][:], axis=AX.X,
            apply_absolute_value=True), [th2], 1))
        sc.dma("gpsimd", h2_bd[i, :, :], h2_st[buf][:], [th2])
        tw = sc.confirm("gpsimd")
        t_h2_p[buf] = tw
        t_h2w.append(tw)
        if i == 0:
            dump("h2d0", h2_bd[0, :, 0:512], [128, 512], BF16, [tw])
        if i == 11:
            dump("h2d11", h2_bd[11, :, 0:512], [128, 512], BF16, [tw])

    pfree(*[f"pgt{t}" for t in range(4)], *[f"put{t}" for t in range(4)])
    pP7b = palloc("pP7b", [1, 128])
    pP7c = palloc("pP7c", [128, 1])
    pbc_cell[0] = palloc("pbc2", [128, 512])
    t_hm1 = sc.op("vector", lambda e: e.tensor_mul(out=h2max_acc[:, 0:NI],
                                                   in0=h2max_acc[:, 0:NI], in1=su_cols[:]),
                  t_h2m + [t_su], 1)
    t_hm2 = sc.op("vector", lambda e: e.reduce_max(out=fold_col[:], in_=h2max_acc[:, 0:NI],
                                                   axis=AX.X), [t_hm1, tsgc], 1)
    t_hm3 = sc.op("tensor", lambda e: e.transpose(out=pP7b[:],
                                                  in_=fold_col[:], identity=ident[:]),
                  [t_hm2, t_inv1c, tsgc, t_gev[NI - 1], t_silu_p[(NI - 1) % 2]], 1)
    t_hm4 = sc.op("vector", lambda e: e.tensor_copy(out=cc_row[0:1, 0:128], in_=pP7b[:]),
                  [t_hm3, tg1], 1)
    t_ccw2 = sc.dma("gpsimd", cc2_i, cc_row[0:1, :], [t_hm4])
    t_cc2 = sc.op("gpsimd", lambda e: e.collective_compute(
        "AllReduce", ALU.max, replica_groups=[[0, 1, 2, 3, 4, 5, 6, 7]],
        ins=[cc2_i.opt()], outs=[cc2_o.opt()]), [t_ccw2], 1)
    sc.dma("gpsimd", cc_row[0:1, :], cc2_o, [t_cc2])
    c_ccr2 = sc.confirm("gpsimd")
    tg2 = sc.op("vector", lambda e: e.reduce_max(out=scal[0:1, GMAX2:GMAX2 + 1],
                                                 in_=cc_row[0:1, :], axis=AX.X), [c_ccr2], 1)
    ts2 = sc.op("vector", lambda e: e.tensor_scalar(
        out=scal[0:1, S2:S2 + 1], in0=scal[0:1, GMAX2:GMAX2 + 1],
        scalar1=scal[0:1, SIG2:SIG2 + 1], scalar2=1.0 / 127.0, op0=ALU.mult, op1=ALU.mult),
        [tg2, t_sig2], 1)
    tiv2 = sc.op("vector", lambda e: e.reciprocal(out=scal[0:1, INV2:INV2 + 1],
                                                  in_=scal[0:1, S2:S2 + 1]), [ts2], 1)
    tivmm2 = sc.op("tensor", lambda e: e.matmul(out=pP7c[:],
                                                lhsT=onesf[:],
                                                rhs=scal[0:1, INV2:INV2 + 1],
                                                start=True, stop=True), [tiv2, t_hm3, t_hm4], 1)
    t_inv2c = sc.op("vector", lambda e: e.tensor_copy(out=fold_col[:], in_=pP7c[:]),
                    [tivmm2, t_hm4], 1)
    t_n2s = sc.op("vector", lambda e: e.tensor_scalar(
        out=su_cols[:], in0=su_cols[:], scalar1=fold_col[:, 0:1], scalar2=None,
        op0=ALU.mult), [t_inv2c, t_hm1], 1)
    dump("scl", scal[0:1, 0:16], [1, 16], F32, [t_n2s, ts2, tiv2])

    sc.dma("sync", csc_row[:], sd_bd, [t_sdw, t_n1f[-1]])
    t_sdr = sc.confirm("sync")
    t_cs1 = sc.op("vector", lambda e: e.tensor_scalar(
        out=csc_row[:], in0=csc_row[:], scalar1=1.0 / 127.0, scalar2=None, op0=ALU.mult),
        [t_sdr], 1)
    t_cs2 = sc.op("vector", lambda e: e.tensor_mul(
        out=csc_row[:], in0=csc_row[:], in1=gb_row[:]), [t_cs1, t_gbrow], 1)
    t_cs3 = sc.op("vector", lambda e: e.tensor_scalar(
        out=csc_row[:], in0=csc_row[:], scalar1=scal[0:1, S2:S2 + 1], scalar2=None,
        op0=ALU.mult), [t_cs2, ts2], 1)
    t_csb = bcast_row(colsc_b, csc_row[:], [t_cs3, tivmm2, t_inv2c] + t_wo_f, f32src=True)
    pfree("pP7b", "pP7c", "pbc2")
    pbc_cell[0] = None
    pdn = [palloc(f"pdn{t}", [128, 512]) for t in range(8)]

    t_y = [None] * NTO
    t_n2p = {}
    t_h2sp = [None, None]
    t_n2bfp = [None, None]
    t_yev_last = [None] * 4
    t_h2l = {}
    t_dfold = {}
    dstr_war = [t_h2w[NI - 1], t_h2m[-1]]

    def issue_h2s(grp, i):
        if (grp, i) in t_h2l or grp >= 4:
            return
        b2 = i % 2
        t_h2l[(grp, i)] = sc.dma(
            "sync", h2s[b2][:], h2_bd[i, :, grp * 512:(grp + 1) * 512],
            [t_h2w[i], t_h2sp[b2]], ch=1 + i % 2)

    for grp in range(4):
        ev_prev = None
        t_x1ld = {}

        def issue_x1(jj, grp=grp):
            if jj in t_x1ld or jj >= 4:
                return
            j = grp * 4 + jj
            t_x1ld[jj] = sc.dma("sync", y_st[2 + jj % 2][:],
                                x1_bd[j * 128:(j + 1) * 128, :],
                                [t_x1w[j], t_hmax[NTO - 1],
                                 (t_y[j - 2] if j >= 2 else None)], ch=3 + jj % 2)

        for i in range(NI):
            b2 = i % 2
            issue_h2s(grp, i)
            if i == 4:
                issue_x1(0)
                issue_x1(1)
            if grp == 0:
                tdl = sc.dmat("sync", dstr_all[i][:], nd_bd[:, i * 128:(i + 1) * 128],
                              [t_nd_w[-1]] + dstr_war, ch=1 + i % 2)
            if i + 1 < NI:
                issue_h2s(grp, i + 1)
            else:
                issue_h2s(grp + 1, 0)
            trd = sc.confirm("sync", at=(tdl if grp == 0 else t_h2l[(grp, i)]))
            ti8 = sc.op("vector", lambda e, i=i, b2=b2: e.tensor_scalar(
                out=n2i8[b2][:], in0=h2s[b2][:], scalar1=su_cols[:, i:i + 1], scalar2=None,
                op0=ALU.mult), [trd, t_n2s, t_n2p.get((grp, i - 2))], 1)
            t_h2sp[b2] = ti8
            tbf = sc.op("vector", lambda e, b2=b2: e.tensor_copy(
                out=n2bf[b2][:], in_=n2i8[b2][:]), [ti8, t_n2bfp[b2]], 1)
            t_n2p[(grp, i)] = tbf
            if grp == 0:
                t_dfold[i] = sc.op("vector", lambda e, i=i: e.tensor_mul(
                    out=dstr_all[i][:], in0=dstr_all[i][:], in1=colsc_b[:]),
                    [trd] + t_csb, 1)
                if i == 0:
                    dump("dsf0", dstr_all[0][:], [128, D], BF16, [t_dfold[0]])
            prev = None
            for jj in range(4):
                for half in range(2):
                    wf = []
                    if i == 0 and jj == 0 and half == 0:
                        wf = [t_yev_last[grp - 1] if grp >= 1 else None, tivmm2]
                    prev = sc.op("tensor", lambda e, i=i, jj=jj, half=half, b2=b2: e.matmul(
                        out=pdn[jj * 2 + half][:],
                        lhsT=n2bf[b2][:, jj * 128:(jj + 1) * 128],
                        rhs=dstr_all[i][:, half * 512:(half + 1) * 512],
                        start=(i == 0), stop=(i == NI - 1)),
                        [tbf, t_dfold[i]] + wf, 1 if (jj == 3 and half == 1) else 0)
            t_n2bfp[b2] = prev
            ev_prev = prev
        tev = None
        for jj in range(4):
            j = grp * 4 + jj
            xr = y_st[2 + jj % 2]
            issue_x1(jj)
            issue_x1(jj + 1)
            tx1 = sc.confirm("sync", at=t_x1ld[jj])
            ybuf = y_st[jj % 2]
            tev0 = sc.op("vector", lambda e, jj=jj, ybuf=ybuf, xr=xr: e.tensor_add(
                out=ybuf[:, 0:512], in0=xr[:, 0:512], in1=pdn[jj * 2][:]),
                [ev_prev, tx1, tev,
                 (t_ln[-1] if j < 2 else None)], 1)
            tev = sc.op("vector", lambda e, jj=jj, ybuf=ybuf, xr=xr: e.tensor_add(
                out=ybuf[:, 512:1024], in0=xr[:, 512:1024], in1=pdn[jj * 2 + 1][:]),
                [tev0], 1)
            t_y[j] = sc.dma("sync", y_out[j * 128:(j + 1) * 128, :], ybuf[:], [tev])
        t_yev_last[grp] = tev
    sc.confirm("sync")
    sc.confirm("gpsimd")
    return nc, sc


def render(nc, sc):
    import contextlib
    with contextlib.ExitStack() as st:
        sems = {}
        for e in ENGS:
            sems[("C", e)] = st.enter_context(nc.semaphore(f"C_{e}"))
        for e in DMAQ:
            for ch in range(Sched.NCH):
                sems[("R", e, ch)] = st.enter_context(nc.semaphore(f"R_{e}{ch}"))
            sems[("D", e)] = st.enter_context(nc.semaphore(f"D_{e}"))
        block = st.enter_context(nc.Block())
        block.sync(sc.render_body("sync", sems))
        block.scalar(sc.render_body("scalar", sems))
        block.vector(sc.render_body("vector", sems))
        block.tensor(sc.render_body("tensor", sems))
        block.gpsimd(sc.render_body("gpsimd", sems))
    return nc


_CACHE = {}


def get_program(dbg=(), stage="full"):
    key = (tuple(sorted(dbg)), stage)
    if key not in _CACHE:
        nc, sc = build_program(dbg, stage)
        _CACHE[key] = render(nc, sc)
    return _CACHE[key]


def make_inputs(inputs, core):
    b, hh = core // 2, core % 2
    x = np.asarray(inputs["L0_attn_x"][b], dtype=np.float32)
    if hh == 1:
        x = np.concatenate([x[T:], x[:T]], axis=0)
    return {
        "x_full": np.ascontiguousarray(x),
        "w_q": np.ascontiguousarray(inputs["L0_attn_w_q"], dtype=np.float32),
        "w_k": np.ascontiguousarray(inputs["L0_attn_w_k"], dtype=np.float32),
        "w_v": np.ascontiguousarray(inputs["L0_attn_w_v"], dtype=np.float32),
        "w_out": np.ascontiguousarray(inputs["L0_attn_w_out"], dtype=np.float32),
        "w_gate": np.ascontiguousarray(inputs["L0_ffn_w_gate"], dtype=np.float32),
        "w_up": np.ascontiguousarray(inputs["L0_ffn_w_up"], dtype=np.float32),
        "w_down": np.ascontiguousarray(inputs["L0_ffn_w_down"], dtype=np.float32),
        "wmod": np.ascontiguousarray(inputs["L0_wmod"], dtype=np.float32),
        "c_b": np.ascontiguousarray(np.asarray(inputs["L0_c"], dtype=np.float32)[b:b + 1]),
        "softmax_scale": np.asarray(inputs["L0_attn_softmax_scale"],
                                    dtype=np.float32).reshape(1, 1),
        "clip_x": np.asarray(inputs["L0_ffn_clip_x"], dtype=np.float32).reshape(1, 1),
        "clip_down": np.asarray(inputs["L0_ffn_clip_down"], dtype=np.float32).reshape(1, 1),
        "identity": np.eye(128, dtype=np.float32),
        "ones_row": np.ones((1, 128), dtype=np.float32),
    }


def kernel(**inputs):
    nc = get_program()
    in_maps = [make_inputs(inputs, c) for c in range(8)]
    res = run_bass_kernel_spmd(nc, in_maps, core_ids=list(range(8)))
    y = np.empty((4, S, D), dtype=np.float32)
    for c in range(8):
        b, hh = c // 2, c % 2
        y[b, hh * T:(hh + 1) * T] = res.results[c]["y"]
    return y


# version 15 (final)
# speedup vs baseline: 1.0158x; 1.0158x over previous
import numpy as np
import concourse.bass as bass
import concourse.mybir as mybir
from concourse.bass_utils import run_bass_kernel_spmd

F32 = mybir.dt.float32
BF16 = mybir.dt.bfloat16
I8 = mybir.dt.int8
U8 = mybir.dt.uint8
AF = mybir.ActivationFunctionType
ALU = mybir.AluOpType
AX = mybir.AxisListType

D, S, T = 1024, 4096, 2048
NH, DK, IF = 16, 64, 2816
NCD, NTO, NTA, NI = 8, 16, 32, 22
EPS = 1e-5
ENGS = ("sync", "scalar", "vector", "tensor", "gpsimd")
DMAQ = ("sync", "scalar", "gpsimd")


class Sched:
    NCH = 5

    def __init__(self):
        self.q = {e: [] for e in ENGS}
        self.ccnt = {e: 0 for e in ENGS}
        self.rcnt = {(e, ch): 0 for e in DMAQ for ch in range(self.NCH)}
        self.dcnt = {e: 0 for e in DMAQ}

    def op(self, eng, fn, wait=(), inc=0):
        self.q[eng].append(("op", fn, [t for t in wait if t is not None], inc))
        if inc:
            self.ccnt[eng] += inc
            return ("C", eng, self.ccnt[eng])
        return None

    def _dma(self, eng, fn, wait, ch):
        self.q[eng].append(("dma", fn, [t for t in wait if t is not None], ch))
        self.rcnt[(eng, ch)] += 16
        return ("R", eng, self.rcnt[(eng, ch)], ch)

    def dma(self, eng, out, in_, wait=(), ch=0):
        return self._dma(eng, lambda e, o=out, i=in_: e.dma_start(out=o, in_=i), wait, ch)

    def dmat(self, eng, out, in_, wait=(), ch=0):
        return self._dma(eng, lambda e, o=out, i=in_: e.dma_start_transpose(out=o, in_=i),
                         wait, ch)

    def confirm(self, eng, wait=(), at=None):
        if at is None:
            targets = [(ch, self.rcnt[(eng, ch)]) for ch in range(self.NCH)
                       if self.rcnt[(eng, ch)]]
        else:
            assert at[0] == "R" and at[1] == eng
            targets = [(at[3], at[2])]
        self.q[eng].append(("confirm", targets, [t for t in wait if t is not None], 1))
        self.dcnt[eng] += 1
        return ("D", eng, self.dcnt[eng])

    def render_body(self, eng_name, sems):
        lw = {}

        def wait_tok(eng_obj, tok):
            kind, e2, c2 = tok[0], tok[1], tok[2]
            if kind == "R" and e2 != eng_name:
                raise AssertionError(
                    f"unsafe cross-engine wait on raw DMA sem {e2} (count {c2}) from "
                    f"{eng_name} at queue idx {getattr(self, '_dbg_idx', None)}")
            key = (kind, e2, tok[3]) if kind == "R" else (kind, e2)
            sem = sems[key]
            if c2 > lw.get(key, 0):
                eng_obj.wait_ge(sem, c2)
                lw[key] = c2

        def emit(eng_obj):
            for idx, (kind, fn, waits, inc) in enumerate(self.q[eng_name]):
                self._dbg_idx = (eng_name, idx)
                if kind == "confirm":
                    for ch, tgt in fn:
                        wait_tok(eng_obj, ("R", eng_name, tgt, ch))
                    for t in waits:
                        wait_tok(eng_obj, t)
                    eng_obj.nop().then_inc(sems[("D", eng_name)], 1)
                    continue
                for t in waits:
                    wait_tok(eng_obj, t)
                ins = fn(eng_obj)
                if kind == "dma":
                    ins.then_inc(sems[("R", eng_name, inc)], 16)
                elif inc:
                    ins.then_inc(sems[("C", eng_name)], inc)
        return emit


def build_program(dbg=(), stage="full"):
    dbg = set(dbg)
    nc = bass.Bass("TRN2", target_bir_lowering=False, detect_race_conditions=False)
    sc = Sched()

    def din(name, shape, dt=F32):
        return nc.dram_tensor(name, shape, dt, kind="ExternalInput").ap()

    x_in = din("x_full", [S, D])
    wq_in = din("w_q", [D, D])
    wk_in = din("w_k", [DK, D])
    wv_in = din("w_v", [DK, D])
    wo_in = din("w_out", [D, D])
    wg_in = din("w_gate", [IF, D])
    wu_in = din("w_up", [IF, D])
    wd_in = din("w_down", [D, IF])
    wmod_in = din("wmod", [D, 6 * D])
    c_in = din("c_b", [1, D])
    ss_in = din("softmax_scale", [1, 1])
    cx_in = din("clip_x", [1, 1])
    cd_in = din("clip_down", [1, 1])
    id_in = din("identity", [128, 128])
    ones_in = din("ones_row", [1, 128])
    y_out = nc.dram_tensor("y", [T, D], F32, kind="ExternalOutput").ap()

    def scr(name, shape, dt):
        return nc.dram_tensor(name, shape, dt, kind="Internal").ap()

    wq_bd = scr("wq_bd", [D, D], BF16)
    wk_bd = scr("wk_bd", [DK, D], BF16)
    wv_bd = scr("wv_bd", [DK, D], BF16)
    wo_bd = scr("wo_bd", [D, D], BF16)
    ng_bd = scr("ng_bd", [IF, D], BF16)
    nu_bd = scr("nu_bd", [IF, D], BF16)
    nd_bd = scr("nd_bd", [D, IF], BF16)
    h1_bd = scr("h1_bd", [S, D], BF16)
    k_bd = scr("k_bd", [DK, S], BF16)
    v_bd = scr("v_bd", [DK, S], BF16)
    attn_bd = scr("attn_bd", [NH, DK, T], BF16)
    htm_bd = scr("htm_bd", [T, D], BF16)
    n1_bd = scr("n1_bd", [T, D], BF16)
    h2_bd = scr("h2_bd", [NI, 128, T], BF16)
    x1_bd = scr("x1_bd", [T, D], F32)
    mrow_bd = scr("mrow_bd", [1, 2 * D], BF16)
    brow_bd = scr("brow_bd", [1, D + 2 * DK], BF16)
    sd_bd = scr("sd_bd", [1, D], F32)
    cc1_i = scr("cc1_i", [1, 128], F32)
    cc1_o = scr("cc1_o", [1, 128], F32)
    cc2_i = scr("cc2_i", [1, 128], F32)
    cc2_o = scr("cc2_o", [1, 128], F32)

    A = nc.alloc_sbuf_tensor

    def region(name, nbytes):
        h = A(name, [128, nbytes], U8, align_bytes=64)
        return nc.lookup_mloc(h).addr

    def at(name, base, off, shape, dt):
        return nc.alloc_sbuf_tensor_at(name, shape, dt, offset=base + off, align_bytes=32)

    RA = region("RA", 16384)
    RB = region("RB", 12544)
    RC = region("RC", 28672)
    RD = region("RD", 32768)
    RE = region("RE", 16384)
    RF = region("RF", 32768)
    RG = region("RG", 8448)
    RH = region("RH", 4096)
    RW = region("RW", 16384)

    x_stream = [at(f"xs_{j}", RA, j * 4096, [128, D], F32) for j in range(2)]
    xt4 = [at(f"xt_{j}", RA, j * 4096, [128, D], F32) for j in range(4)]
    y_st = [at(f"yst_{j}", RA, j * 4096, [128, D], F32) for j in range(4)]

    wmod_sb = at("wmod_sb", RB, 0, [128, 6 * D], BF16)
    wmod_sbB = at("wmod_sbB", RF, 0, [128, 6 * D], BF16)
    k2 = at("k2", RB, 0, [128, S], BF16)
    v_tok = at("v_tok", RB, 8192, [128, NTA * DK], BF16)
    h2s = [at(f"h2s_{j}", RB, j * 1024, [128, 512], BF16) for j in range(2)]
    n2i8 = [at(f"n2i8_{j}", RB, 2048 + j * 512, [128, 512], I8) for j in range(2)]
    n2bf = [at(f"n2bf_{j}", RB, 4096 + j * 1024, [128, 512], BF16) for j in range(2)]

    mod_row = at("mod_row", RC, 0, [1, 6 * D], BF16)
    srow_g = at("srow_g", RC, 12288, [1, 2 * D], BF16)
    brow = at("brow", RC, 16384, [1, D + 2 * DK], BF16)
    wn_ld = [at(f"wnld_{j}", RC, j * 11264, [128, IF], F32) for j in range(2)]
    wn_i8 = [at(f"wni8_{j}", RC, 22528 + j * 2816, [128, IF], I8) for j in range(2)]
    dstr = [at(f"dstr_{j}", RC, 8192 + j * 2048, [128, D], BF16) for j in range(2)]

    h1f = [at(f"h1f_{c}", RD, c * 4096, [128, T], BF16) for c in range(NCD)]
    n1_fm = [at(f"n1f_{c}", RD, c * 4096, [128, T], BF16) for c in range(NCD)]

    w_qT = at("w_qT", RE, 0, [128, NCD * D], BF16)
    attn_lhs = [at(f"alhs_{c}", RE, c * 1024, [128, 512], BF16) for c in range(NCD)]
    silu_sb = [at(f"silu_{j}", RE, j * 4096, [128, T], BF16) for j in range(2)]
    h2_st = [at(f"h2st_{j}", RE, 8192 + j * 4096, [128, T], BF16) for j in range(2)]

    q_fm = [at(f"qfm_{c}", RF, c * 4096, [128, T], BF16) for c in range(NCD)]
    dstr_all = [at(f"dsta_{i}", RF if i < 16 else RE,
                   (i * 2048 if i < 16 else (i - 16) * 2048), [128, D], BF16)
                for i in range(NI)]
    gres = [at(f"gres_{c}", RF, c * 2816, [128, 1408], BF16) for c in range(NCD)]
    ures = [at(f"ures_{c}",
               RW if c < 5 else (RG if c < 7 else RH),
               (c * 2816 if c < 5 else ((c - 5) * 2816 if c < 7 else 0)),
               [128, 1408], BF16) for c in range(NCD)]

    v1 = at("v1", RG, 0, [128, NTA * 65], BF16)
    P_sb = [at(f"P_{j}", RG, 4224 + j * 2048, [128, 1024], BF16) for j in range(2)]

    w_kT = at("w_kT", RH, 0, [128, NCD * DK], BF16)
    w_vT = at("w_vT", RH, 1024, [128, NCD * DK], BF16)
    w_kvT = at("w_kvT", RH, 2048, [128, NCD * 128], BF16)
    w_oT = at("w_oT", RW, 0, [128, NCD * D], BF16)

    ident = A("ident", [128, 128], F32)
    onesb = A("onesb", [1, 128], BF16)
    onesf = A("onesf", [1, 128], F32)
    onesf_col = A("onesf_col", [128, 1], F32)
    eps_col = A("eps_col", [128, 1], F32)
    scal = A("scal", [1, 16], F32)
    SS, SIG1, SIG2, TMPA, GMAX1, S1, INV1, GMAX2, S2, INV2, MAXK2, TMPB, WQC = range(13)
    c_row = A("c_row", [8, 128], F32)
    c_row2 = A("c_row2", [8, 128], F32)
    c_cols = A("c_cols", [128, 8], BF16)
    mod_cols = A("mod_cols", [128, 16], F32)
    mod_cols_bf = A("mod_cols_bf", [128, 16], BF16)
    bias_cols = A("bias_cols", [128, 9], F32)
    bq_ss = A("bq_ss", [128, 8], F32)
    ca1ss = A("ca1ss", [128, 8], F32)
    sq_col = A("sq_col", [128, 1], F32)
    fold_t = A("fold_t", [16, 128], BF16)
    fold_f = A("fold_f", [16, 128], F32)
    gb_row = A("gb_row", [1, D], F32)
    csc_row = A("csc_row", [1, D], F32)
    cb1_b = A("cb1_b", [128, D], BF16)
    sb_b = A("sb_b", [128, D], BF16)
    ga_b = A("ga_b", [128, D], BF16)
    colsc_b = A("colsc_b", [128, D], BF16)
    bq2_row = A("bq2_row", [1, 8], F32)
    negB_row = A("negB_row", [1, 8], F32)
    negB_cols = A("negB_cols", [128, 8], F32)
    q_sq = A("q_sq", [128, D], F32)
    kv_stage = [A(f"kvs_{j}", [128, 512], BF16) for j in range(2)]
    rd_bf = A("rd_bf", [1, 512], BF16)
    ssrow_bf = A("ssrow_bf", [1, 1], BF16)
    tpv = [A(f"tpv_{j}", [64, 512], BF16) for j in range(2)]
    attn_stage = [A(f"astg_{j}", [64, 512], BF16) for j in range(3)]
    h1t = [A(f"h1t_{j}", [128, D], BF16) for j in range(4)]
    htm_res = [at(f"htm_{j}", RD, j * 2048, [128, D], BF16) for j in range(NTO)]
    stats = [A(f"stats_{j}", [128, 2], F32) for j in range(4)]
    bn6 = [A(f"bn6_{j}", [128, 12], F32) for j in range(4)]
    rstd = [A(f"rstd_{j}", [128, 1], F32) for j in range(4)]
    hmax_acc = A("hmax_acc", [128, NTO], F32)
    h2max_acc = A("h2max_acc", [128, NI], F32)
    fold_col = A("fold_col", [128, 1], F32)
    cc_row = A("cc_row", [1, 128], F32)
    inv1_col = A("inv1_col", [128, 1], F32)
    wmax = A("wmax", [128, 2 * NI + NCD], F32)
    winv = A("winv", [128, 1], F32)
    sg_cols = A("sg_cols", [128, NI], F32)
    su_cols = A("su_cols", [128, NI], F32)
    n1_i8 = [A(f"n1i8_{j}", [128, D], I8) for j in range(2)]

    _pguards = {}
    _porder = {}

    def palloc(name, shape, dt=F32):
        g = nc.psum_tensor(name, shape, dt)
        h = g.__enter__()
        _pguards[name] = g
        _porder[name] = len(_porder)
        return h

    def pfree(*names):
        for n in sorted(names, key=lambda n: -_porder[n]):
            _pguards.pop(n).__exit__(None, None, None)

    def dump(tag, ap, shape, dt=F32, wait=()):
        if tag not in dbg:
            return None
        o = nc.dram_tensor(f"dbg_{tag}", shape, dt, kind="ExternalOutput").ap()
        return sc.dma("gpsimd", o, ap, wait=wait)

    def finalize(deps):
        cs = sc.confirm("sync")
        cg = sc.confirm("gpsimd")
        safe = [t for t in deps if t is not None and t[0] != "R"] + [cs, cg]
        tm = sc.op("vector", lambda e: e.memset(y_st[0][:], 0.0), safe, 1)
        for j in range(NTO):
            sc.dma("sync", y_out[j * 128:(j + 1) * 128, :], y_st[0][:], [tm])
        sc.confirm("sync")
        return nc, sc

    t_ones = sc.dma("gpsimd", onesb[:], ones_in)
    t_wqc = sc.dma("gpsimd", wq_bd, wq_in)
    t_wkc = sc.dma("gpsimd", wk_bd, wk_in)
    t_wvc = sc.dma("gpsimd", wv_bd, wv_in)
    t_woc = sc.dma("gpsimd", wo_bd, wo_in)
    c_wcast = sc.confirm("gpsimd")
    c_ones = c_wcast
    t_id = sc.dma("sync", ident[:], id_in)
    t_cr = sc.dma("sync", c_row[:], c_in.rearrange("o (p a) -> (o p) a", p=8))
    t_ss = sc.dma("sync", scal[0:1, SS:SS + 1], ss_in)
    t_cx = sc.dma("sync", scal[0:1, TMPA:TMPA + 1], cx_in)
    c_cx = sc.confirm("sync")
    t_sig1 = sc.op("scalar", lambda e: e.activation(out=scal[0:1, SIG1:SIG1 + 1],
                                                    in_=scal[0:1, TMPA:TMPA + 1],
                                                    func=AF.Sigmoid), [c_cx], 1)
    t_cd = sc.dma("sync", scal[0:1, TMPA:TMPA + 1], cd_in, [t_sig1])
    c_cd = sc.confirm("sync")
    t_sig2 = sc.op("scalar", lambda e: e.activation(out=scal[0:1, SIG2:SIG2 + 1],
                                                    in_=scal[0:1, TMPA:TMPA + 1],
                                                    func=AF.Sigmoid), [c_cd], 1)
    t_c1 = sc.op("vector", lambda e: e.memset(onesf[:], 1.0), (), 1)
    t_c2 = sc.op("vector", lambda e: e.memset(onesf_col[:], 1.0), (), 1)
    t_c3 = sc.op("vector", lambda e: e.memset(eps_col[:], EPS), (), 1)
    t_c5 = sc.op("vector", lambda e: e.memset(cc_row[:], 0.0), (), 1)

    t_csil = sc.op("scalar", lambda e: e.activation(out=c_row2[:], in_=c_row[:],
                                                    func=AF.Silu), [c_cx], 1)
    pTc = palloc("pTc", [128, 8])
    t_cct = sc.op("tensor", lambda e: e.transpose(out=pTc[:], in_=c_row2[:],
                                                  identity=ident[0:8, 0:8]), [t_csil, c_cx], 1)
    t_ccol = sc.op("vector", lambda e: e.tensor_copy(out=c_cols[:], in_=pTc[:]),
                   [t_cct], 1)
    pfree("pTc")

    if stage == "m_a":
        return finalize([t_ccol, t_sig2, t_c5, t_c1, t_c2, t_c3, t_ones])

    pm0 = palloc("pm0", [1, 2048])
    pm1 = palloc("pm1", [1, 2048])
    pmt = {0: pm0, 1: pm1, 2: pm0}
    t_modev = []
    wbufs = [wmod_sb, wmod_sbB]
    t_wmod_war = [None, None]
    t_wload = {}

    def issue_wmod(k):
        if k in t_wload or k >= 16:
            return
        a = k % NCD
        if k < 8:
            t_wload[k] = sc.dma("gpsimd", wbufs[k % 2][:, 0:4096],
                                wmod_in[a * 128:(a + 1) * 128, 0:4096],
                                [t_wmod_war[k % 2]], ch=1 + k % 2)
        else:
            t_wload[k] = sc.dma("gpsimd", wbufs[k % 2][:, 0:2048],
                                wmod_in[a * 128:(a + 1) * 128, 4096:6144],
                                [t_wmod_war[k % 2]], ch=1 + k % 2)

    for pnum, groups in enumerate([(0, 1), (2,)]):
        glast = {}
        gl_idx = groups[-1]
        for a in range(NCD):
            k = pnum * NCD + a
            wb = wbufs[k % 2]
            issue_wmod(k)
            issue_wmod(k + 1)
            w8 = sc.confirm("gpsimd", at=t_wload[k])
            extra = t_modev[:2] if (pnum == 1 and a == 0) else []
            for g in groups:
                goff = g * 2048 - (4096 if pnum == 1 else 0)
                for ns in range(4):
                    glast[g] = sc.op("tensor", lambda e, a=a, ns=ns, wb=wb, g=g,
                                     goff=goff: e.matmul(
                        out=pmt[g][0:1, ns * 512:(ns + 1) * 512],
                        lhsT=c_cols[:, a:a + 1],
                        rhs=wb[:, goff + ns * 512:goff + (ns + 1) * 512],
                        start=(a == 0), stop=(a == NCD - 1)),
                        [w8, t_ccol] + extra,
                        1 if (ns == 3 and (g == gl_idx or a == NCD - 1)) else 0)
            t_wmod_war[k % 2] = glast[gl_idx]
        for g in groups:
            t_modev.append(sc.op("vector", lambda e, g=g: e.tensor_copy(
                out=mod_row[0:1, g * 2048:(g + 1) * 2048],
                in_=pmt[g][0:1, :]), [glast[g]], 1))
    pfree("pm0", "pm1")
    t_mod_all = list(t_modev)

    if stage == "m_b":
        return finalize(t_mod_all)

    t_ca1 = sc.op("vector", lambda e: e.tensor_scalar(
        out=mod_row[0:1, D:2 * D], in0=mod_row[0:1, D:2 * D], scalar1=1.0, scalar2=None,
        op0=ALU.add), t_mod_all, 1)
    t_cb1 = sc.op("vector", lambda e: e.tensor_scalar(
        out=mod_row[0:1, 4 * D:5 * D], in0=mod_row[0:1, 4 * D:5 * D], scalar1=1.0,
        scalar2=None, op0=ALU.add), t_mod_all, 1)
    t_g = {}
    for name, off in (("ga", 2 * D), ("gb", 5 * D)):
        dep = [t_g["ga"]] if name == "gb" else []
        t1 = sc.op("vector", lambda e, off=off: e.tensor_scalar(
            out=srow_g[0:1, 0:D], in0=mod_row[0:1, off:off + D], scalar1=0.0, scalar2=2.0,
            op0=ALU.is_ge, op1=ALU.mult), t_mod_all + dep, 1)
        t1b = sc.op("vector", lambda e: e.tensor_scalar(
            out=srow_g[0:1, 0:D], in0=srow_g[0:1, 0:D], scalar1=1.0, scalar2=None,
            op0=ALU.subtract), [t1], 1)
        t2 = sc.op("scalar", lambda e, off=off: e.activation(
            out=srow_g[0:1, D:2 * D], in_=mod_row[0:1, off:off + D], func=AF.Abs),
            t_mod_all + dep + [t1], 1)
        t3 = sc.op("vector", lambda e: e.tensor_scalar(
            out=srow_g[0:1, D:2 * D], in0=srow_g[0:1, D:2 * D], scalar1=0.25, scalar2=None,
            op0=ALU.max), [t2], 1)
        t_g[name] = sc.op("vector", lambda e, off=off: e.tensor_mul(
            out=mod_row[0:1, off:off + D], in0=srow_g[0:1, 0:D], in1=srow_g[0:1, D:2 * D]),
            [t1b, t3], 1)
    t_ga, t_gb = t_g["ga"], t_g["gb"]
    t_gbrow = sc.op("vector", lambda e: e.tensor_copy(out=gb_row[:],
                                                      in_=mod_row[0:1, 5 * D:6 * D]),
                    [t_gb], 1)
    t_mod_fin = [t_ca1, t_cb1, t_ga, t_gb] + t_mod_all
    dump("mod_row", mod_row[:], [1, 6 * D], BF16, t_mod_fin + [t_gbrow])

    if stage == "m_c":
        return finalize(t_mod_fin + [t_gbrow])

    t_mw = sc.dma("sync", mrow_bd, mod_row[0:1, 0:2 * D], [t_ca1] + t_mod_all)
    t_mr2 = sc.dma("sync", fold_t[:], mrow_bd.rearrange("o (p a) -> (o p) a", p=16), [t_mw])
    c_mr2 = sc.confirm("sync")
    t_foldf = sc.op("vector", lambda e: e.tensor_copy(out=fold_f[:], in_=fold_t[:]),
                    [c_mr2], 1)
    pT16 = palloc("pT16", [128, 16])
    t_mt = sc.op("tensor", lambda e: e.transpose(out=pT16[:], in_=fold_f[:],
                                                 identity=ident[0:16, 0:16]),
                 [t_foldf, t_ccol] + t_mod_all, 1)
    t_mcols = sc.op("vector", lambda e: e.tensor_copy(out=mod_cols[:], in_=pT16[:]),
                    [t_mt], 1)
    t_mcolsbf = sc.op("vector", lambda e: e.tensor_copy(out=mod_cols_bf[:], in_=mod_cols[:]),
                      [t_mcols], 1)
    pfree("pT16")

    t_ssb = sc.op("vector", lambda e: e.tensor_copy(out=ssrow_bf[:], in_=scal[0:1, SS:SS + 1]),
                  [c_cx], 1)
    pss = palloc("pss", [128, 1])
    t_ssmm = sc.op("tensor", lambda e: e.matmul(out=pss[:], lhsT=onesb[:],
                                                rhs=ssrow_bf[:], start=True, stop=True),
                   [t_ssb, c_ones] + t_mod_all, 1)
    t_sscol = sc.op("vector", lambda e: e.tensor_copy(out=sq_col[:], in_=pss[:]),
                    [t_ssmm], 1)
    pfree("pss")
    t_ca1ss = sc.op("vector", lambda e: e.tensor_scalar(
        out=ca1ss[:], in0=mod_cols[:, 8:16], scalar1=sq_col[:, 0:1], scalar2=None,
        op0=ALU.mult), [t_mcols, t_sscol], 1)

    bcast_last = [t_sscol]
    pbc_cell = [palloc("pbc", [128, 512])]

    def bcast_row(dst, row_ap, deps, n=D, f32src=False):
        toks = []
        lh = onesf if f32src else onesb
        pt = pbc_cell[0]
        for ns in range((n + 511) // 512):
            w = min(512, n - ns * 512)
            tmm = sc.op("tensor", lambda e, ns=ns, w=w, lh=lh, pt=pt: e.matmul(
                out=pt[:, 0:w], lhsT=lh[:], rhs=row_ap[0:1, ns * 512:ns * 512 + w],
                start=True, stop=True),
                list(deps) + [t_c1, c_ones, t_sscol, bcast_last[0]], 1)
            tev = sc.op("vector", lambda e, dst=dst, ns=ns, w=w, pt=pt: e.tensor_copy(
                out=dst[:, ns * 512:ns * 512 + w], in_=pt[:, 0:w]), [tmm], 1)
            bcast_last[0] = tev
            toks.append(tev)
        return toks

    t_cb1b = bcast_row(cb1_b, mod_row[0:1, 4 * D:5 * D], [t_cb1])
    t_sbb = bcast_row(sb_b, mod_row[0:1, 3 * D:4 * D], t_mod_all)
    t_gab = bcast_row(ga_b, mod_row[0:1, 2 * D:3 * D], [t_ga])

    if stage == "mod":
        return finalize([t_gbrow, t_mcolsbf, t_ca1ss] + t_cb1b + t_sbb + t_gab)

    t_ln, t_h1w, t_xl = [], [], []
    t_xload = {}

    def emit_x_load(j):
        if j in t_xload or j >= 32:
            return
        wl = [t_ln[j - 2]] if j >= 2 else []
        t_xload[j] = sc.dma("sync", x_stream[j % 2][:], x_in[j * 128:(j + 1) * 128, :],
                            wl, ch=1 + j % 2)

    def emit_ln_tile(j):
        buf = x_stream[j % 2]
        r = j % 4
        emit_x_load(j)
        emit_x_load(j + 1)
        tx = sc.confirm("sync", at=t_xload[j])
        t_xl.append(tx)
        wb = [t_ln[j - 4]] if j >= 4 else []
        wh = [t_h1w[j - 4]] if j >= 4 else []
        sc.op("vector", lambda e, b=buf, r=r: e.bn_stats(out=bn6[r][:, 0:6], in_=b[:, 0:512]),
              [tx] + wb, 0)
        tb2 = sc.op("vector", lambda e, b=buf, r=r: e.bn_stats(out=bn6[r][:, 6:12],
                                                               in_=b[:, 512:1024]), [], 1)
        tagg = sc.op("vector", lambda e, r=r: e.bn_aggr(out=stats[r][:], in_=bn6[r][:]),
                     [tb2], 1)
        tsq = sc.op("scalar", lambda e, r=r: e.activation(out=rstd[r][:], in_=stats[r][:, 1:2],
                                                          func=AF.Sqrt, bias=eps_col[:, 0:1],
                                                          scale=1.0), [tagg, t_c3], 1)
        trc = sc.op("vector", lambda e, r=r: e.reciprocal(out=rstd[r][:], in_=rstd[r][:]),
                    [tsq], 1)
        tap = sc.op("vector", lambda e, b=buf, r=r: e.tensor_scalar(
            out=h1t[r][:], in0=b[:], scalar1=stats[r][:, 0:1], scalar2=rstd[r][:, 0:1],
            op0=ALU.subtract, op1=ALU.mult), [trc, tx] + wh, 1)
        t_ln.append(tap)
        sc.dma("gpsimd", h1_bd[j * 128:(j + 1) * 128, :], h1t[r][:], [tap])
        t_h1w.append(sc.confirm("gpsimd"))

    for j in range(2):
        emit_ln_tile(j)

    t_wqT, t_wkTl, t_wvTl, t_woTl = [], [], [], []
    for c in range(NCD):
        t_wqT.append(sc.dmat("sync", w_qT[:, c * D:(c + 1) * D],
                             wq_bd[:, c * 128:(c + 1) * 128], [c_wcast]))
        t_wkTl.append(sc.dmat("sync", w_kT[:, c * DK:(c + 1) * DK],
                              wk_bd[:, c * 128:(c + 1) * 128], [c_wcast]))
        t_wvTl.append(sc.dmat("sync", w_vT[:, c * DK:(c + 1) * DK],
                              wv_bd[:, c * 128:(c + 1) * 128], [c_wcast]))
        t_woTl.append(sc.dmat("sync", w_oT[:, c * D:(c + 1) * D],
                              wo_bd[:, c * 128:(c + 1) * 128], [c_wcast]))

    c_wT = sc.confirm("sync")
    c_woT = c_wT
    dump("w_kT_raw", w_kT[:], [128, NCD * DK], BF16, [c_wT])
    dump("w_vT_raw", w_vT[:], [128, NCD * DK], BF16, [c_wT])
    dump("w_qT_raw", w_qT[:, 0:D], [128, D], BF16, [c_wT])
    dump("wk_bd_d", wk_bd, [DK, D], BF16, [c_wT])
    dump("wv_bd_d", wv_bd, [DK, D], BF16, [c_wT])
    t_bias_mm = []
    pbias = palloc("pbias", [1, D + 2 * DK])
    for half in range(2):
        prev = None
        for c in range(NCD):
            prev = sc.op("tensor", lambda e, c=c, h=half: e.matmul(
                out=pbias[0:1, h * 512:(h + 1) * 512],
                lhsT=mod_cols_bf[:, c:c + 1],
                rhs=w_qT[:, c * D + h * 512:c * D + (h + 1) * 512],
                start=(c == 0), stop=(c == NCD - 1)),
                [t_mcolsbf, c_wT], 1 if c == NCD - 1 else 0)
        t_bias_mm.append(prev)
    for wt, toks, col0 in ((w_kT, t_wkTl, 1024), (w_vT, t_wvTl, 1024 + DK)):
        prev = None
        for c in range(NCD):
            prev = sc.op("tensor", lambda e, c=c, wt=wt, col0=col0: e.matmul(
                out=pbias[0:1, col0:col0 + DK],
                lhsT=mod_cols_bf[:, c:c + 1],
                rhs=wt[:, c * DK:(c + 1) * DK],
                start=(c == 0), stop=(c == NCD - 1)),
                [t_mcolsbf, c_wT], 1 if c == NCD - 1 else 0)
        t_bias_mm.append(prev)
    t_bev = sc.op("vector", lambda e: e.tensor_copy(out=brow[0:1, :],
                                                    in_=pbias[0:1, 0:D + 2 * DK]),
                  t_bias_mm, 1)
    pfree("pbias")
    t_bw = sc.dma("sync", brow_bd, brow[0:1, :], [t_bev])
    t_br = sc.dma("sync", fold_t[0:9, :], brow_bd.rearrange("o (p a) -> (o p) a", p=9),
                  [t_bw, t_foldf])
    c_br = sc.confirm("sync")
    t_foldf2 = sc.op("vector", lambda e: e.tensor_copy(out=fold_f[0:9, :], in_=fold_t[0:9, :]),
                     [c_br, t_mcols], 1)
    pT9 = palloc("pT9", [128, 9])
    t_bt = sc.op("tensor", lambda e: e.transpose(out=pT9[:], in_=fold_f[0:9, :],
                                                 identity=ident[0:9, 0:9]),
                 [t_foldf2, t_mt], 1)
    t_bcols = sc.op("vector", lambda e: e.tensor_copy(out=bias_cols[:], in_=pT9[:]),
                    [t_bt], 1)
    pfree("pT9")
    t_bqss = sc.op("vector", lambda e: e.tensor_scalar(
        out=bq_ss[:], in0=bias_cols[:, 0:8], scalar1=sq_col[:, 0:1], scalar2=None,
        op0=ALU.mult), [t_bcols, t_sscol], 1)

    t_wq_s = []
    for c in range(NCD):
        t_wq_s.append(sc.op("vector", lambda e, c=c: e.tensor_scalar(
            out=w_qT[:, c * D:(c + 1) * D], in0=w_qT[:, c * D:(c + 1) * D],
            scalar1=ca1ss[:, c:c + 1], scalar2=None, op0=ALU.mult),
            [t_ca1ss, c_wT] + t_bias_mm[:2], 1))
    t_kv_p = []
    for c in range(NCD):
        sc.op("vector", lambda e, c=c: e.tensor_scalar(
            out=w_kvT[:, c * 128:c * 128 + DK], in0=w_kT[:, c * DK:(c + 1) * DK],
            scalar1=mod_cols[:, 8 + c:9 + c], scalar2=None, op0=ALU.mult),
            [c_wT, t_mcols] + t_bias_mm[2:], 0)
        t_kv_p.append(sc.op("vector", lambda e, c=c: e.tensor_scalar(
            out=w_kvT[:, c * 128 + DK:(c + 1) * 128], in0=w_vT[:, c * DK:(c + 1) * DK],
            scalar1=mod_cols[:, 8 + c:9 + c], scalar2=None, op0=ALU.mult),
            [c_wT], 1))
    t_wo_f = []
    for c in range(NCD):
        t_wo_f.append(sc.op("vector", lambda e, c=c: e.tensor_mul(
            out=w_oT[:, c * D:(c + 1) * D], in0=w_oT[:, c * D:(c + 1) * D], in1=ga_b[:]),
            [c_woT] + t_gab, 1))

    dump("w_kT_late", w_kT[:], [128, NCD * DK], BF16, t_kv_p + t_bias_mm)
    dump("mod_cols", mod_cols[:], [128, 16], F32, [t_mcols])
    dump("brow", brow[:], [1, D + 2 * DK], BF16, [t_bev])
    dump("bias_cols", bias_cols[:], [128, 9], F32, [t_bcols])
    dump("w_qT0", w_qT[:, 0:D], [128, D], BF16, [t_wq_s[0]])
    dump("w_kvT0", w_kvT[:, 0:128], [128, 128], BF16, [t_kv_p[0]])

    if stage == "wprep":
        return finalize([t_bqss] + t_wq_s + t_kv_p + t_wo_f)

    c_bw = sc.confirm("sync")
    mod_region_free = t_cb1b + t_sbb + t_gab + [t_gbrow, t_bcols, c_bw]
    qs_state = {"ld": [None, None], "i8": [None, None]}
    t_ng_w, t_nu_w, t_nd_w = [], [], []

    def quant_chunk_emit(src, n_bd, i, m0, ncols, out_list, first):
        buf = (m0 + i) % 2
        sc.dma("gpsimd", wn_ld[buf][:, 0:ncols], src[i * 128:(i + 1) * 128, :],
               [qs_state["ld"][buf]] + (mod_region_free if first else []))
        tl = sc.confirm("gpsimd")
        tm = sc.op("vector", lambda e, buf=buf, i=i, m0=m0, nc_=ncols: e.reduce_max(
            out=wmax[:, m0 + i:m0 + i + 1], in_=wn_ld[buf][:, 0:nc_], axis=AX.X,
            apply_absolute_value=True), [tl], 1)
        tiv = sc.op("vector", lambda e, i=i, m0=m0: e.reciprocal(
            out=winv[:], in_=wmax[:, m0 + i:m0 + i + 1]), [tm], 1)
        ti8 = sc.op("vector", lambda e, buf=buf, nc_=ncols: e.tensor_scalar(
            out=wn_i8[buf][:, 0:nc_], in0=wn_ld[buf][:, 0:nc_], scalar1=winv[:, 0:1],
            scalar2=127.0, op0=ALU.mult, op1=ALU.mult), [tiv, qs_state["i8"][buf]], 1)
        qs_state["ld"][buf] = ti8
        sc.dma("gpsimd", n_bd[i * 128:(i + 1) * 128, :], wn_i8[buf][:, 0:ncols], [ti8])
        tw = sc.confirm("gpsimd")
        qs_state["i8"][buf] = tw
        out_list.append(tw)

    quant_steps = []
    for i in range(NI):
        quant_steps.append(lambda i=i: quant_chunk_emit(wg_in, ng_bd, i, 0, D, t_ng_w, i < 2))
    for i in range(NI):
        quant_steps.append(lambda i=i: quant_chunk_emit(wu_in, nu_bd, i, NI, D, t_nu_w, False))
    for i in range(NCD):
        quant_steps.append(lambda i=i: quant_chunk_emit(wd_in, nd_bd, i, 2 * NI, IF,
                                                        t_nd_w, False))
    quant_steps.reverse()

    def quant_tick(n=1):
        for _ in range(n):
            if quant_steps:
                quant_steps.pop()()

    for j in range(2, 16):
        emit_ln_tile(j)


    emit_x_load(16)
    emit_x_load(17)
    t_h1f = [None] * NCD
    for c in range(NCD):
        t_h1f[c] = sc.dmat("sync", h1f[c][:], h1_bd[0:T, c * 128:(c + 1) * 128],
                           [t_h1w[15]])
    c_h1f0 = sc.confirm("sync")
    dump("h1f00", h1f[0][:, 0:512], [128, 512], BF16, [c_h1f0])
    for j in range(16, 32):
        emit_ln_tile(j)

    if stage == "ln":
        return finalize(t_h1f + [t_h1w[-1]])

    t_kvev, t_kbw, t_vbw = [], [], []
    pfree("pbc")
    pbc_cell[0] = None
    pkv = [palloc(f"pkv{t}", [128, 512]) for t in range(4)]
    pq = [palloc(f"pq{t}", [128, 512]) for t in range(4)]

    def emit_kv_half(half, h1f_toks):
        mml = [None] * 4
        for c in range(NCD):
            for ts in range(4):
                mml[ts] = sc.op("tensor", lambda e, c=c, ts=ts: e.matmul(
                    out=pkv[ts][:],
                    lhsT=w_kvT[:, c * 128:(c + 1) * 128],
                    rhs=h1f[c][:, ts * 512:(ts + 1) * 512],
                    start=(c == 0), stop=(c == NCD - 1)),
                    [h1f_toks[c], t_kv_p[c],
                     (t_kvev[(half - 1) * 4 + ts] if half > 0 else None)],
                    1 if c == NCD - 1 else 0)
        for ts in range(4):
            gts = half * 4 + ts
            stg = kv_stage[gts % 2]
            wstg = [t_vbw[gts - 2]] if gts >= 2 else []
            tev = sc.op("scalar", lambda e, ts=ts, stg=stg: e.activation(
                out=stg[:], in_=pkv[ts][:], func=AF.Identity, bias=bias_cols[:, 8:9],
                scale=1.0), [mml[ts], t_bcols] + wstg, 1)
            t_kvev.append(tev)
            sc.dma("gpsimd", k_bd[:, gts * 512:(gts + 1) * 512], stg[0:64, :], [tev])
            sc.dma("gpsimd", v_bd[:, gts * 512:(gts + 1) * 512], stg[64:128, :], [tev])
            t_kbw.append(sc.confirm("gpsimd"))
            t_vbw.append(t_kbw[-1])
            wm = t_mod_all if gts == 0 else []
            sc.dma("sync", k2[0:64, gts * 512:(gts + 1) * 512],
                   k_bd[:, gts * 512:(gts + 1) * 512], [t_kbw[-1]] + wm)
            sc.dma("sync", k2[64:128, gts * 512:(gts + 1) * 512],
                   k_bd[:, gts * 512:(gts + 1) * 512], [])
            for k4 in range(4):
                kt = gts * 4 + k4
                sc.dmat("sync", v_tok[:, kt * DK:(kt + 1) * DK],
                        v_bd[:, kt * 128:(kt + 1) * 128], wm if k4 == 0 else [])

    emit_kv_half(0, [c_h1f0] * NCD)
    t_qev = []
    for pair in range(NCD):
        mml = [None] * 4
        for c in range(NCD):
            for ts in range(4):
                mml[ts] = sc.op("tensor", lambda e, pair=pair, c=c, ts=ts: e.matmul(
                    out=pq[ts][:],
                    lhsT=w_qT[:, c * D + pair * 128:c * D + (pair + 1) * 128],
                    rhs=h1f[c][:, ts * 512:(ts + 1) * 512],
                    start=(c == 0), stop=(c == NCD - 1)),
                    [c_h1f0, t_wq_s[c], t_bev,
                     (t_qev[(pair - 1) * 4 + ts] if pair > 0 else None)],
                    1 if c == NCD - 1 else 0)
        for ts in range(4):
            t_qev.append(sc.op("scalar", lambda e, pair=pair, ts=ts: e.activation(
                out=q_fm[pair][:, ts * 512:(ts + 1) * 512], in_=pq[ts][:], func=AF.Identity,
                bias=bq_ss[:, pair:pair + 1], scale=1.0),
                [mml[ts], t_bqss], 1))
        quant_tick(2)
    dump("qfm0", q_fm[0][:, 0:512], [128, 512], BF16, [t_qev[3]])

    for c in range(NCD):
        sc.dmat("sync", h1f[c][:], h1_bd[T:2 * T, c * 128:(c + 1) * 128],
                [t_h1w[31], t_qev[-1]])
    c_h1f1 = sc.confirm("sync")
    emit_kv_half(1, [c_h1f1] * NCD)

    c_kv2 = sc.confirm("sync")
    t_c4 = sc.op("vector", lambda e: e.memset(v1[:], 1.0), (), 1)
    t_v1 = []
    for kt in range(NTA):
        t_v1.append(sc.op("vector", lambda e, kt=kt: e.tensor_copy(
            out=v1[:, kt * 65:kt * 65 + DK], in_=v_tok[:, kt * DK:(kt + 1) * DK]),
            [c_kv2, t_c4], 1 if kt % 8 == 7 else 0))
    t_v1 = [t for t in t_v1 if t]
    dump("k2", k2[:, 0:512], [128, 512], BF16, [c_kv2])
    dump("v1", v1[:, 0:650], [128, 650], BF16, t_v1)

    t_qdone = t_qev[-1]
    pfree(*[f"pkv{t}" for t in range(4)], *[f"pq{t}" for t in range(4)])
    pbnd = palloc("pbnd", [1, 512])
    pnegb = palloc("pnegb", [128, 8])
    t_prev = None
    t_maxk = None
    for tsl in range(8):
        tq = sc.op("scalar", lambda e, tsl=tsl: e.activation(
            out=q_sq[0:64, 0:512], in_=k2[0:64, tsl * 512:(tsl + 1) * 512],
            func=AF.Square), [c_kv2, t_prev, t_qdone], 1)
        mm = sc.op("tensor", lambda e: e.matmul(
            out=pbnd[:], lhsT=onesf_col[0:64, :], rhs=q_sq[0:64, 0:512],
            start=True, stop=True), [tq, t_c2, t_qdone, t_prev], 1)
        if tsl == 0:
            t_maxk = sc.op("vector", lambda e: e.reduce_max(
                out=scal[0:1, MAXK2:MAXK2 + 1], in_=pbnd[:], axis=AX.X), [mm], 1)
        else:
            tr_ = sc.op("vector", lambda e: e.reduce_max(
                out=scal[0:1, TMPB:TMPB + 1], in_=pbnd[:], axis=AX.X),
                [mm, t_maxk], 1)
            t_maxk = sc.op("vector", lambda e: e.tensor_tensor(
                out=scal[0:1, MAXK2:MAXK2 + 1], in0=scal[0:1, MAXK2:MAXK2 + 1],
                in1=scal[0:1, TMPB:TMPB + 1], op=ALU.max), [tr_], 1)
        t_prev = t_maxk
    t_bq2 = []
    t_prev_mm = t_maxk
    for pair in range(NCD):
        t_acc = None
        for qtr in range(4):
            tq = sc.op("scalar", lambda e, pair=pair, qtr=qtr: e.activation(
                out=q_sq[:, 0:512], in_=q_fm[pair][:, qtr * 512:(qtr + 1) * 512],
                func=AF.Square), [t_qev[pair * 4 + 3], t_prev_mm], 1)
            mm = sc.op("tensor", lambda e: e.matmul(
                out=pbnd[:], lhsT=onesf_col[:], rhs=q_sq[:, 0:512],
                start=True, stop=True), [tq, t_prev_mm], 1)
            if qtr == 0:
                t_acc = sc.op("vector", lambda e, pair=pair: e.reduce_max(
                    out=bq2_row[0:1, pair:pair + 1], in_=pbnd[:], axis=AX.X),
                    [mm], 1)
            else:
                tr_ = sc.op("vector", lambda e: e.reduce_max(
                    out=scal[0:1, TMPB:TMPB + 1], in_=pbnd[:], axis=AX.X),
                    [mm, t_acc], 1)
                t_acc = sc.op("vector", lambda e, pair=pair: e.tensor_tensor(
                    out=bq2_row[0:1, pair:pair + 1], in0=bq2_row[0:1, pair:pair + 1],
                    in1=scal[0:1, TMPB:TMPB + 1], op=ALU.max), [tr_], 1)
            t_prev_mm = t_acc
        t_bq2.append(t_acc)
    t_b1 = sc.op("vector", lambda e: e.tensor_scalar(
        out=bq2_row[:], in0=bq2_row[:], scalar1=scal[0:1, MAXK2:MAXK2 + 1], scalar2=None,
        op0=ALU.mult), t_bq2 + [t_maxk], 1)
    t_b2 = sc.op("scalar", lambda e: e.activation(out=negB_row[:], in_=bq2_row[:],
                                                  func=AF.Sqrt), [t_b1], 1)
    t_b3 = sc.op("vector", lambda e: e.tensor_scalar(
        out=negB_row[:], in0=negB_row[:], scalar1=-1.0, scalar2=None, op0=ALU.mult),
        [t_b2], 1)
    t_b4 = sc.op("tensor", lambda e: e.matmul(out=pnegb[:], lhsT=onesf[:],
                                              rhs=negB_row[:], start=True, stop=True),
                 [t_b3, t_c1, t_prev_mm], 1)
    t_negB = sc.op("vector", lambda e: e.tensor_copy(out=negB_cols[:], in_=pnegb[:]),
                   [t_b4], 1)
    pfree("pbnd", "pnegb")
    dump("negB", negB_cols[:], [128, 8], F32, [t_negB])

    if stage == "qkv":
        return finalize([t_negB, c_kv2] + t_v1 + [t_qev[-1]])

    qkv_done = [t_negB, c_kv2]
    psc = [palloc("pscA", [128, 1024]), palloc("pscB", [128, 1024])]
    ppv = [palloc("ppv0", [65, 512]), palloc("ppv1", [65, 512])]
    pnorm = palloc("pnorm", [64, 512])
    popj = palloc("popj", [128, 512])
    t_ostg_dma = {}
    stage_confirms = {}
    t_pair_done = {}
    t_x1 = [None] * NTO
    t_x1w = [None] * NTO
    t_hmax = [None] * NTO
    t_opj_last = [None]
    t_prev_tbc = [None]
    prev_state = {"exp": {}, "pv": {}, "evacs": []}

    def emit_attn_pair(qb, pair):
        q0 = qb * 512
        pv_toks = {}
        t_exp = {}
        pexp, ppv_prev = prev_state["exp"], prev_state["pv"]

        def emit_score_exp(kt):
            pS = psc[kt % 2]
            if kt >= 2:
                w_sc = [t_exp[kt - 2]]
            else:
                w_sc = [pexp.get(30 + kt)] if pexp else list(qkv_done)
            sc.op("tensor", lambda e, kt=kt, pS=pS: e.matmul(
                out=pS[:, 0:512],
                lhsT=k2[0:64, kt * 128:(kt + 1) * 128],
                rhs=q_fm[pair][0:64, q0:q0 + 512],
                start=True, stop=True, tile_position=(0, 0)), w_sc, 0)
            mm1 = sc.op("tensor", lambda e, kt=kt, pS=pS: e.matmul(
                out=pS[:, 512:1024],
                lhsT=k2[64:128, kt * 128:(kt + 1) * 128],
                rhs=q_fm[pair][64:128, q0:q0 + 512],
                start=True, stop=True, tile_position=(64, 0)), [], 1)
            if kt >= 2:
                w_ex = [mm1, pv_toks[kt - 2]]
            else:
                w_ex = [mm1] + ([ppv_prev.get(30 + kt)] if ppv_prev else [t_negB])
            t_exp[kt] = sc.op("scalar", lambda e, kt=kt, pS=pS: e.activation(
                out=P_sb[kt % 2][:], in_=pS[:], func=AF.Exp,
                bias=negB_cols[:, pair:pair + 1], scale=1.0), w_ex, 1)

        def emit_pv(kt):
            w_pv = [t_exp[kt]] + ((prev_state["evacs"] or t_v1) if kt == 0 else [])
            sc.op("tensor", lambda e, kt=kt: e.matmul(
                out=ppv[0][:], lhsT=v1[:, kt * 65:(kt + 1) * 65],
                rhs=P_sb[kt % 2][:, 0:512], start=(kt == 0), stop=(kt == NTA - 1)), w_pv, 0)
            pv_toks[kt] = sc.op("tensor", lambda e, kt=kt: e.matmul(
                out=ppv[1][:], lhsT=v1[:, kt * 65:(kt + 1) * 65],
                rhs=P_sb[kt % 2][:, 512:1024], start=(kt == 0), stop=(kt == NTA - 1)), [], 1)

        emit_score_exp(0)
        for kt in range(1, NTA):
            emit_score_exp(kt)
            emit_pv(kt - 1)
        emit_pv(NTA - 1)
        evacs = []
        t_prev_norm = None
        for hl in range(2):
            h = pair * 2 + hl
            stg = attn_stage[(qb * 16 + h) % 3]
            widx = (qb * 16 + h) - 3
            wstg = []
            if widx >= 0:
                qb_o, p_o = divmod(widx, 16)[0], (widx % 16) // 2
                if (qb_o, p_o) in stage_confirms:
                    wstg = [stage_confirms[(qb_o, p_o)]]
            def _recip(e, hl=hl):
                with nc.allow_low_precision("bf16 softmax denominators (honest-bf16 path)"):
                    return e.reciprocal(out=rd_bf[0:1, :], in_=ppv[hl][64:65, :])
            tr_ = sc.op("vector", _recip, [pv_toks[NTA - 1], t_prev_tbc[0]], 1)
            tcp = sc.op("scalar", lambda e, hl=hl: e.activation(
                out=tpv[hl][:], in_=ppv[hl][0:64, :], func=AF.Identity,
                bias=0.0, scale=1.0), [tr_], 1)
            tbc = sc.op("tensor", lambda e: e.matmul(
                out=pnorm[:], lhsT=onesb[0:1, 0:64],
                rhs=rd_bf[0:1, :], start=True, stop=True), [tr_, t_prev_norm], 1)
            t_prev_tbc[0] = tbc
            tml = sc.op("vector", lambda e, hl=hl, stg=stg: e.tensor_mul(
                out=stg[:], in0=tpv[hl][:], in1=pnorm[:]),
                [tcp, tbc] + wstg, 1)
            t_prev_norm = tml
            t_ostg_dma[(qb, h)] = sc.dma("sync", attn_bd[h, :, qb * 512:(qb + 1) * 512],
                                         stg[:], [tml])
            evacs.extend([tr_, tcp])
        t_pair_done[(qb, pair)] = sc.confirm("sync")
        stage_confirms[(qb, pair)] = t_pair_done[(qb, pair)]
        prev_state["exp"] = t_exp
        prev_state["pv"] = pv_toks
        prev_state["evacs"] = list(evacs)

    c_al_of = {}

    def emit_al_loads(qb):
        al_deps = [t_pair_done[(qb, NCD - 1)]]
        if qb == 0:
            al_deps += [t_qev[-1]]
        for c in range(NCD):
            wprev = [t_opj_last[0]] if t_opj_last[0] is not None else []
            sc.dma("sync", attn_lhs[c][:],
                   attn_bd.rearrange("h d t -> (h d) t")[
                       c * 128:(c + 1) * 128, qb * 512:(qb + 1) * 512],
                   al_deps + wprev)
        c_al_of[qb] = sc.confirm("sync")

    def emit_post_block(qb):
        c_al = c_al_of[qb]
        t_xld = {}

        def issue_x(jj):
            if jj in t_xld or jj >= 4:
                return
            j = qb * 4 + jj
            wxb = [t_ln[-1]] if qb == 0 else []
            t_xld[jj] = sc.dma("gpsimd", xt4[jj % 2][:], x_in[j * 128:(j + 1) * 128, :],
                               wxb + ([t_x1[j - 2]] if j >= 2 else []), ch=1 + jj % 2)

        for jj in range(4):
            j = qb * 4 + jj
            xb = xt4[jj % 2]
            x1b = xt4[2 + jj % 2]
            issue_x(jj)
            issue_x(jj + 1)
            tx = sc.confirm("gpsimd", at=t_xld[jj])
            prev_ev = None
            for half in range(2):
                prev = None
                for c in range(NCD):
                    prev = sc.op("tensor", lambda e, c=c, jj=jj, half=half: e.matmul(
                        out=popj[:],
                        lhsT=attn_lhs[c][:, jj * 128:(jj + 1) * 128],
                        rhs=w_oT[:, c * D + half * 512:c * D + (half + 1) * 512],
                        start=(c == 0), stop=(c == NCD - 1)),
                        [c_al, prev_ev,
                         (t_opj_last[0] if (half == 0 and c == 0) else None)] +
                        (t_wo_f if (qb == 0 and jj == 0) else []),
                        1 if c == NCD - 1 else 0)
                prev_ev = sc.op("vector", lambda e, half=half, xb=xb, x1b=x1b: e.tensor_add(
                    out=x1b[:, half * 512:(half + 1) * 512],
                    in0=xb[:, half * 512:(half + 1) * 512],
                    in1=popj[:]), [prev, tx], 1)
            t_x1[j] = prev_ev
            t_x1w[j] = sc.dma("sync", x1_bd[j * 128:(j + 1) * 128, :], x1b[:], [prev_ev])
            t_opj_last[0] = prev_ev
            r = j % 4
            stg = htm_res[j]
            sc.op("vector", lambda e, b=x1b, r=r: e.bn_stats(out=bn6[r][:, 0:6],
                                                             in_=b[:, 0:512]), [prev_ev], 0)
            tb2 = sc.op("vector", lambda e, b=x1b, r=r: e.bn_stats(out=bn6[r][:, 6:12],
                                                                   in_=b[:, 512:1024]), [], 1)
            tagg = sc.op("vector", lambda e, r=r: e.bn_aggr(out=stats[r][:], in_=bn6[r][:]),
                         [tb2], 1)
            tsq = sc.op("scalar", lambda e, r=r: e.activation(
                out=rstd[r][:], in_=stats[r][:, 1:2], func=AF.Sqrt, bias=eps_col[:, 0:1],
                scale=1.0), [tagg], 1)
            trc = sc.op("vector", lambda e, r=r: e.reciprocal(out=rstd[r][:], in_=rstd[r][:]),
                        [tsq], 1)
            tap = sc.op("vector", lambda e, b=x1b, r=r: e.tensor_scalar(
                out=q_sq[:], in0=b[:], scalar1=stats[r][:, 0:1], scalar2=rstd[r][:, 0:1],
                op0=ALU.subtract, op1=ALU.mult), [trc], 1)
            tm1 = sc.op("vector", lambda e: e.tensor_mul(
                out=q_sq[:], in0=q_sq[:], in1=cb1_b[:]), [tap] + t_cb1b, 1)
            tm2 = sc.op("vector", lambda e, stg=stg: e.tensor_add(
                out=stg[:], in0=q_sq[:], in1=sb_b[:]),
                [tm1, tx, c_h1f1, t_kvev[-1], t_qev[-1]] + t_sbb, 1)
            t_hmax[j] = sc.op("vector", lambda e, stg=stg, j=j: e.reduce_max(
                out=hmax_acc[:, j:j + 1], in_=stg[:], axis=AX.X,
                apply_absolute_value=True), [tm2], 1)

    for qb in range(4):
        for pair in range(NCD):
            emit_attn_pair(qb, pair)
            quant_tick(2)
            if pair == 1 and qb >= 1:
                emit_post_block(qb - 1)
        emit_al_loads(qb)
    emit_post_block(3)
    quant_tick(60)
    dump("attn00", attn_bd[0, :, 0:512], [DK, 512], BF16, [t_pair_done[(0, 0)]])
    dump("attn30", attn_bd[0, :, 1536:2048], [DK, 512], BF16, [t_pair_done[(3, 0)]])
    dump("attn01", attn_bd[1, :, 0:512], [DK, 512], BF16, [t_pair_done[(0, 0)]])
    dump("x1_0", xt4[2][:, 0:D], [128, D], F32, [t_x1[14]])

    if stage == "attn":
        return finalize([t_hmax[-1]] + t_nd_w)

    pfree("pscA", "pscB", "ppv0", "ppv1", "pnorm", "popj")
    pP5a = palloc("pP5a", [8, 128])
    pP5b = palloc("pP5b", [1, 128])
    pP5c = palloc("pP5c", [128, 1])
    pP5d = palloc("pP5d", [128, 1])
    t_sdt = sc.op("tensor", lambda e: e.transpose(
        out=pP5a[:], in_=wmax[:, 2 * NI:2 * NI + NCD],
        identity=ident[:]), t_nd_w + [t_pair_done[(3, 7)], t_x1[NTO - 1]], 1)
    t_sdc = sc.op("vector", lambda e: e.tensor_copy(out=fold_f[0:8, :], in_=pP5a[:]),
                  [t_sdt, t_foldf2], 1)
    sc.dma("gpsimd", sd_bd.rearrange("o (p a) -> (o p) a", p=8), fold_f[0:8, :],
           [t_sdc])
    t_sdw = sc.confirm("gpsimd")

    tf1 = sc.op("vector", lambda e: e.reduce_max(out=fold_col[:], in_=hmax_acc[:],
                                                 axis=AX.X), list(t_hmax), 1)
    tf2 = sc.op("tensor", lambda e: e.transpose(out=pP5b[:],
                                                in_=fold_col[:], identity=ident[:]),
                [tf1, t_sdc], 1)
    tf3 = sc.op("vector", lambda e: e.tensor_copy(out=cc_row[0:1, 0:128], in_=pP5b[:]),
                [tf2, t_c5], 1)
    t_ccw = sc.dma("gpsimd", cc1_i, cc_row[0:1, :], [tf3])
    t_cc1 = sc.op("gpsimd", lambda e: e.collective_compute(
        "AllReduce", ALU.max, replica_groups=[[0, 1, 2, 3, 4, 5, 6, 7]],
        ins=[cc1_i.opt()], outs=[cc1_o.opt()]), [t_ccw], 1)
    sc.dma("gpsimd", cc_row[0:1, :], cc1_o, [t_cc1])
    c_ccr = sc.confirm("gpsimd")
    tg1 = sc.op("vector", lambda e: e.reduce_max(out=scal[0:1, GMAX1:GMAX1 + 1],
                                                 in_=cc_row[0:1, :], axis=AX.X), [c_ccr], 1)
    ts1 = sc.op("vector", lambda e: e.tensor_scalar(
        out=scal[0:1, S1:S1 + 1], in0=scal[0:1, GMAX1:GMAX1 + 1],
        scalar1=scal[0:1, SIG1:SIG1 + 1], scalar2=1.0 / 127.0, op0=ALU.mult, op1=ALU.mult),
        [tg1, t_sig1], 1)
    tiv1 = sc.op("vector", lambda e: e.reciprocal(out=scal[0:1, INV1:INV1 + 1],
                                                  in_=scal[0:1, S1:S1 + 1]), [ts1], 1)
    tivmm = sc.op("tensor", lambda e: e.matmul(out=pP5c[:],
                                               lhsT=onesf[:],
                                               rhs=scal[0:1, INV1:INV1 + 1],
                                               start=True, stop=True), [tiv1, tf3], 1)
    t_inv1c = sc.op("vector", lambda e: e.tensor_copy(out=inv1_col[:], in_=pP5c[:]),
                    [tivmm], 1)
    t_n1w = []
    t_n1wr = {}
    for j in range(NTO):
        buf = j % 2
        wb = [t_n1w[j - 2]] if j >= 2 else []
        ti = sc.op("vector", lambda e, buf=buf, j=j: e.tensor_scalar(
            out=n1_i8[buf][:], in0=htm_res[j][:], scalar1=inv1_col[:, 0:1], scalar2=None,
            op0=ALU.mult), [t_inv1c, t_hmax[j]] + wb, 1)
        t_n1wr[j] = sc.dma("gpsimd", n1_bd[j * 128:(j + 1) * 128, :], n1_i8[buf][:],
                           [ti], ch=1 + j % 2)
        if j >= 1:
            t_n1w.append(sc.confirm("gpsimd", at=t_n1wr[j - 1]))
    t_n1w.append(sc.confirm("gpsimd", at=t_n1wr[NTO - 1]))
    c_n1w = t_n1w[-1]
    c_n1f_h = [None, None]
    for hh2 in range(2):
        for c in range(NCD):
            sc.dmat("sync", n1_fm[c][:, hh2 * 1024:(hh2 + 1) * 1024],
                    n1_bd[hh2 * 1024:(hh2 + 1) * 1024, c * 128:(c + 1) * 128],
                    [t_n1w[max(8 * hh2 + 7, 2 * c + hh2)], t_qev[-1], t_kvev[-1]])
        c_n1f_h[hh2] = sc.confirm("sync")
    c_n1f = c_n1f_h[1]
    t_n1f = [c_n1f] * NCD
    dump("n1f0", n1_fm[0][:, 0:512], [128, 512], BF16, [c_n1f])
    dump("hmaxd", hmax_acc[:, 0:NTO], [128, NTO], F32, t_hmax)

    if stage == "n1":
        return finalize([c_n1f])

    tsgs = sc.op("vector", lambda e: e.tensor_scalar(
        out=scal[0:1, WQC:WQC + 1], in0=scal[0:1, GMAX1:GMAX1 + 1],
        scalar1=scal[0:1, SIG1:SIG1 + 1], scalar2=1.0 / (127.0 * 127.0),
        op0=ALU.mult, op1=ALU.mult), [tg1], 1)
    tsgmm = sc.op("tensor", lambda e: e.matmul(out=pP5d[:],
                                               lhsT=onesf[:], rhs=scal[0:1, WQC:WQC + 1],
                                               start=True, stop=True),
                  [tsgs, tivmm, t_inv1c], 1)
    tsgc = sc.op("vector", lambda e: e.tensor_copy(
        out=fold_col[:], in_=pP5d[:]), [tsgmm, tf1], 1)
    t_sg = sc.op("vector", lambda e: e.tensor_scalar(
        out=sg_cols[:], in0=wmax[:, 0:NI], scalar1=fold_col[:, 0:1], scalar2=None,
        op0=ALU.mult), [tsgc] + t_ng_w, 1)
    t_su = sc.op("vector", lambda e: e.tensor_scalar(
        out=su_cols[:], in0=wmax[:, NI:2 * NI], scalar1=fold_col[:, 0:1], scalar2=None,
        op0=ALU.mult), [tsgc] + t_nu_w, 1)
    pfree("pP5a", "pP5b", "pP5c", "pP5d")
    pgt = [palloc(f"pgt{t}", [128, 512]) for t in range(4)]
    put = [palloc(f"put{t}", [128, 512]) for t in range(4)]

    t_h2w, t_h2m = [], []
    t_silu_p = [None, None]
    t_h2_p = [None, None]
    t_gev = {}
    t_h2mul = {}
    attn_end = [t_x1[NTO - 1], t_hmax[NTO - 1], prev_state["pv"][NTA - 1]]
    c_ghalf = [None, None]

    def emit_gu_half(h):
        lo = h * 11 * 128
        war = attn_end if h == 0 else [t_gev[10], t_h2mul[10]]
        extra = [t_nd_w[-1], t_n1f[-1]] if h == 0 else []
        last = None
        for c in range(NCD):
            sc.dmat("sync", gres[c][:], ng_bd[lo:lo + 1408, c * 128:(c + 1) * 128],
                    [t_ng_w[h * 11 + 10]] + war + extra)
            last = sc.dmat("sync", ures[c][:], nu_bd[lo:lo + 1408, c * 128:(c + 1) * 128],
                           [t_nu_w[h * 11 + 10]] + war)
        c_ghalf[h] = sc.confirm("sync", at=last)

    emit_gu_half(0)
    for i in range(NI):
        buf = i % 2
        hf, io = i // 11, i % 11
        if i == 11:
            emit_gu_half(1)
        c_gu = c_ghalf[hf]
        tg_rd = [c_gu] * NCD
        tu_rd = [c_gu] * NCD
        g_last = u_last = None
        for ts4 in range(4):
            for c in range(NCD):
                wfirst = []
                if c == 0 and ts4 == 0:
                    wfirst = [t_gev.get(i - 1)]
                    if i == 0:
                        wfirst += [tivmm, tsgmm, t_b4]
                g_last = sc.op("tensor", lambda e, c=c, ts4=ts4, io=io: e.matmul(
                    out=pgt[ts4][:], lhsT=gres[c][:, io * 128:(io + 1) * 128],
                    rhs=n1_fm[c][:, ts4 * 512:(ts4 + 1) * 512],
                    start=(c == 0), stop=(c == NCD - 1)),
                    [tg_rd[c], c_n1f_h[ts4 // 2]] + wfirst,
                    1 if (c == NCD - 1 and ts4 == 3) else 0)
        for ts4 in range(4):
            for c in range(NCD):
                wfirst = []
                if c == 0 and ts4 == 0:
                    wfirst = [t_h2_p[buf], t_silu_p[(i - 1) % 2] if i >= 1 else None]
                u_last = sc.op("tensor", lambda e, c=c, ts4=ts4, io=io: e.matmul(
                    out=put[ts4][:], lhsT=ures[c][:, io * 128:(io + 1) * 128],
                    rhs=n1_fm[c][:, ts4 * 512:(ts4 + 1) * 512],
                    start=(c == 0), stop=(c == NCD - 1)),
                    [tu_rd[c]] + wfirst, 1 if (c == NCD - 1 and ts4 == 3) else 0)
        tsil = None
        for ts4 in range(4):
            tsil = sc.op("scalar", lambda e, i=i, buf=buf, ts4=ts4: e.activation(
                out=silu_sb[buf][:, ts4 * 512:(ts4 + 1) * 512], in_=pgt[ts4][:],
                func=AF.Silu, bias=0.0, scale=sg_cols[:, i:i + 1]),
                [g_last, t_sg, t_silu_p[buf]] + ([t_x1[NTO - 1]] if i < 2 else []),
                1 if ts4 == 3 else 0)
        t_gev[i] = tsil
        th2 = None
        for ts4 in range(4):
            th2 = sc.op("vector", lambda e, buf=buf, ts4=ts4: e.tensor_mul(
                out=h2_st[buf][:, ts4 * 512:(ts4 + 1) * 512],
                in0=silu_sb[buf][:, ts4 * 512:(ts4 + 1) * 512], in1=put[ts4][:]),
                [tsil, u_last, t_h2_p[buf]], 1 if ts4 == 3 else 0)
        t_h2mul[i] = th2
        t_silu_p[buf] = th2
        t_h2m.append(sc.op("vector", lambda e, i=i, buf=buf: e.reduce_max(
            out=h2max_acc[:, i:i + 1], in_=h2_st[buf][:], axis=AX.X,
            apply_absolute_value=True), [th2], 1))
        sc.dma("gpsimd", h2_bd[i, :, :], h2_st[buf][:], [th2])
        tw = sc.confirm("gpsimd")
        t_h2_p[buf] = tw
        t_h2w.append(tw)
        if i == 0:
            dump("h2d0", h2_bd[0, :, 0:512], [128, 512], BF16, [tw])
        if i == 11:
            dump("h2d11", h2_bd[11, :, 0:512], [128, 512], BF16, [tw])

    pfree(*[f"pgt{t}" for t in range(4)], *[f"put{t}" for t in range(4)])
    pP7b = palloc("pP7b", [1, 128])
    pP7c = palloc("pP7c", [128, 1])
    pbc_cell[0] = palloc("pbc2", [128, 512])
    t_hm1 = sc.op("vector", lambda e: e.tensor_mul(out=h2max_acc[:, 0:NI],
                                                   in0=h2max_acc[:, 0:NI], in1=su_cols[:]),
                  t_h2m + [t_su], 1)
    t_hm2 = sc.op("vector", lambda e: e.reduce_max(out=fold_col[:], in_=h2max_acc[:, 0:NI],
                                                   axis=AX.X), [t_hm1, tsgc], 1)
    t_hm3 = sc.op("tensor", lambda e: e.transpose(out=pP7b[:],
                                                  in_=fold_col[:], identity=ident[:]),
                  [t_hm2, t_inv1c, tsgc, t_gev[NI - 1], t_silu_p[(NI - 1) % 2]], 1)
    t_hm4 = sc.op("vector", lambda e: e.tensor_copy(out=cc_row[0:1, 0:128], in_=pP7b[:]),
                  [t_hm3, tg1], 1)
    t_ccw2 = sc.dma("gpsimd", cc2_i, cc_row[0:1, :], [t_hm4])
    t_cc2 = sc.op("gpsimd", lambda e: e.collective_compute(
        "AllReduce", ALU.max, replica_groups=[[0, 1, 2, 3, 4, 5, 6, 7]],
        ins=[cc2_i.opt()], outs=[cc2_o.opt()]), [t_ccw2], 1)
    sc.dma("gpsimd", cc_row[0:1, :], cc2_o, [t_cc2])
    c_ccr2 = sc.confirm("gpsimd")
    tg2 = sc.op("vector", lambda e: e.reduce_max(out=scal[0:1, GMAX2:GMAX2 + 1],
                                                 in_=cc_row[0:1, :], axis=AX.X), [c_ccr2], 1)
    ts2 = sc.op("vector", lambda e: e.tensor_scalar(
        out=scal[0:1, S2:S2 + 1], in0=scal[0:1, GMAX2:GMAX2 + 1],
        scalar1=scal[0:1, SIG2:SIG2 + 1], scalar2=1.0 / 127.0, op0=ALU.mult, op1=ALU.mult),
        [tg2, t_sig2], 1)
    tiv2 = sc.op("vector", lambda e: e.reciprocal(out=scal[0:1, INV2:INV2 + 1],
                                                  in_=scal[0:1, S2:S2 + 1]), [ts2], 1)
    tivmm2 = sc.op("tensor", lambda e: e.matmul(out=pP7c[:],
                                                lhsT=onesf[:],
                                                rhs=scal[0:1, INV2:INV2 + 1],
                                                start=True, stop=True), [tiv2, t_hm3, t_hm4], 1)
    t_inv2c = sc.op("vector", lambda e: e.tensor_copy(out=fold_col[:], in_=pP7c[:]),
                    [tivmm2, t_hm4], 1)
    t_n2s = sc.op("vector", lambda e: e.tensor_scalar(
        out=su_cols[:], in0=su_cols[:], scalar1=fold_col[:, 0:1], scalar2=None,
        op0=ALU.mult), [t_inv2c, t_hm1], 1)
    dump("scl", scal[0:1, 0:16], [1, 16], F32, [t_n2s, ts2, tiv2])

    sc.dma("sync", csc_row[:], sd_bd, [t_sdw, t_n1f[-1]])
    t_sdr = sc.confirm("sync")
    t_cs1 = sc.op("vector", lambda e: e.tensor_scalar(
        out=csc_row[:], in0=csc_row[:], scalar1=1.0 / 127.0, scalar2=None, op0=ALU.mult),
        [t_sdr], 1)
    t_cs2 = sc.op("vector", lambda e: e.tensor_mul(
        out=csc_row[:], in0=csc_row[:], in1=gb_row[:]), [t_cs1, t_gbrow], 1)
    t_cs3 = sc.op("vector", lambda e: e.tensor_scalar(
        out=csc_row[:], in0=csc_row[:], scalar1=scal[0:1, S2:S2 + 1], scalar2=None,
        op0=ALU.mult), [t_cs2, ts2], 1)
    t_csb = bcast_row(colsc_b, csc_row[:], [t_cs3, tivmm2, t_inv2c] + t_wo_f, f32src=True)
    pfree("pP7b", "pP7c", "pbc2")
    pbc_cell[0] = None
    pdn = [palloc(f"pdn{t}", [128, 512]) for t in range(8)]

    t_y = [None] * NTO
    t_n2p = {}
    t_h2sp = [None, None]
    t_n2bfp = [None, None]
    t_yev_last = [None] * 4
    t_h2l = {}
    t_dfold = {}
    dstr_war = [t_h2w[NI - 1], t_h2m[-1]]

    def issue_h2s(grp, i):
        if (grp, i) in t_h2l or grp >= 4:
            return
        b2 = i % 2
        t_h2l[(grp, i)] = sc.dma(
            "sync", h2s[b2][:], h2_bd[i, :, grp * 512:(grp + 1) * 512],
            [t_h2w[i], t_h2sp[b2]], ch=1 + i % 2)

    for grp in range(4):
        ev_prev = None
        t_x1ld = {}

        def issue_x1(jj, grp=grp):
            if jj in t_x1ld or jj >= 4:
                return
            j = grp * 4 + jj
            t_x1ld[jj] = sc.dma("sync", y_st[2 + jj % 2][:],
                                x1_bd[j * 128:(j + 1) * 128, :],
                                [t_x1w[j], t_hmax[NTO - 1],
                                 (t_y[j - 2] if j >= 2 else None)], ch=3 + jj % 2)

        for i in range(NI):
            b2 = i % 2
            issue_h2s(grp, i)
            if i == 4:
                issue_x1(0)
                issue_x1(1)
            if grp == 0:
                tdl = sc.dmat("sync", dstr_all[i][:], nd_bd[:, i * 128:(i + 1) * 128],
                              [t_nd_w[-1]] + dstr_war, ch=1 + i % 2)
            if i + 1 < NI:
                issue_h2s(grp, i + 1)
            else:
                issue_h2s(grp + 1, 0)
            trd = sc.confirm("sync", at=(tdl if grp == 0 else t_h2l[(grp, i)]))
            ti8 = sc.op("vector", lambda e, i=i, b2=b2: e.tensor_scalar(
                out=n2i8[b2][:], in0=h2s[b2][:], scalar1=su_cols[:, i:i + 1], scalar2=None,
                op0=ALU.mult), [trd, t_n2s, t_n2p.get((grp, i - 2))], 1)
            t_h2sp[b2] = ti8
            tbf = sc.op("vector", lambda e, b2=b2: e.tensor_copy(
                out=n2bf[b2][:], in_=n2i8[b2][:]), [ti8, t_n2bfp[b2]], 1)
            t_n2p[(grp, i)] = tbf
            if grp == 0:
                t_dfold[i] = sc.op("vector", lambda e, i=i: e.tensor_mul(
                    out=dstr_all[i][:], in0=dstr_all[i][:], in1=colsc_b[:]),
                    [trd] + t_csb, 1)
                if i == 0:
                    dump("dsf0", dstr_all[0][:], [128, D], BF16, [t_dfold[0]])
            prev = None
            for jj in range(4):
                for half in range(2):
                    wf = []
                    if i == 0 and jj == 0 and half == 0:
                        wf = [t_yev_last[grp - 1] if grp >= 1 else None, tivmm2]
                    prev = sc.op("tensor", lambda e, i=i, jj=jj, half=half, b2=b2: e.matmul(
                        out=pdn[jj * 2 + half][:],
                        lhsT=n2bf[b2][:, jj * 128:(jj + 1) * 128],
                        rhs=dstr_all[i][:, half * 512:(half + 1) * 512],
                        start=(i == 0), stop=(i == NI - 1)),
                        [tbf, t_dfold[i]] + wf, 1 if (jj == 3 and half == 1) else 0)
            t_n2bfp[b2] = prev
            ev_prev = prev
        tev = None
        for jj in range(4):
            j = grp * 4 + jj
            xr = y_st[2 + jj % 2]
            issue_x1(jj)
            issue_x1(jj + 1)
            tx1 = sc.confirm("sync", at=t_x1ld[jj])
            ybuf = y_st[jj % 2]
            tev0 = sc.op("vector", lambda e, jj=jj, ybuf=ybuf, xr=xr: e.tensor_add(
                out=ybuf[:, 0:512], in0=xr[:, 0:512], in1=pdn[jj * 2][:]),
                [ev_prev, tx1, tev,
                 (t_ln[-1] if j < 2 else None)], 1)
            tev = sc.op("vector", lambda e, jj=jj, ybuf=ybuf, xr=xr: e.tensor_add(
                out=ybuf[:, 512:1024], in0=xr[:, 512:1024], in1=pdn[jj * 2 + 1][:]),
                [tev0], 1)
            t_y[j] = sc.dma("sync", y_out[j * 128:(j + 1) * 128, :], ybuf[:], [tev])
        t_yev_last[grp] = tev
    sc.confirm("sync")
    sc.confirm("gpsimd")
    return nc, sc


def render(nc, sc):
    import contextlib
    with contextlib.ExitStack() as st:
        sems = {}
        for e in ENGS:
            sems[("C", e)] = st.enter_context(nc.semaphore(f"C_{e}"))
        for e in DMAQ:
            for ch in range(Sched.NCH):
                sems[("R", e, ch)] = st.enter_context(nc.semaphore(f"R_{e}{ch}"))
            sems[("D", e)] = st.enter_context(nc.semaphore(f"D_{e}"))
        block = st.enter_context(nc.Block())
        block.sync(sc.render_body("sync", sems))
        block.scalar(sc.render_body("scalar", sems))
        block.vector(sc.render_body("vector", sems))
        block.tensor(sc.render_body("tensor", sems))
        block.gpsimd(sc.render_body("gpsimd", sems))
    return nc


_CACHE = {}


def get_program(dbg=(), stage="full"):
    key = (tuple(sorted(dbg)), stage)
    if key not in _CACHE:
        nc, sc = build_program(dbg, stage)
        _CACHE[key] = render(nc, sc)
    return _CACHE[key]


def make_inputs(inputs, core):
    b, hh = core // 2, core % 2
    x = np.asarray(inputs["L0_attn_x"][b], dtype=np.float32)
    if hh == 1:
        x = np.concatenate([x[T:], x[:T]], axis=0)
    return {
        "x_full": np.ascontiguousarray(x),
        "w_q": np.ascontiguousarray(inputs["L0_attn_w_q"], dtype=np.float32),
        "w_k": np.ascontiguousarray(inputs["L0_attn_w_k"], dtype=np.float32),
        "w_v": np.ascontiguousarray(inputs["L0_attn_w_v"], dtype=np.float32),
        "w_out": np.ascontiguousarray(inputs["L0_attn_w_out"], dtype=np.float32),
        "w_gate": np.ascontiguousarray(inputs["L0_ffn_w_gate"], dtype=np.float32),
        "w_up": np.ascontiguousarray(inputs["L0_ffn_w_up"], dtype=np.float32),
        "w_down": np.ascontiguousarray(inputs["L0_ffn_w_down"], dtype=np.float32),
        "wmod": np.ascontiguousarray(inputs["L0_wmod"], dtype=np.float32),
        "c_b": np.ascontiguousarray(np.asarray(inputs["L0_c"], dtype=np.float32)[b:b + 1]),
        "softmax_scale": np.asarray(inputs["L0_attn_softmax_scale"],
                                    dtype=np.float32).reshape(1, 1),
        "clip_x": np.asarray(inputs["L0_ffn_clip_x"], dtype=np.float32).reshape(1, 1),
        "clip_down": np.asarray(inputs["L0_ffn_clip_down"], dtype=np.float32).reshape(1, 1),
        "identity": np.eye(128, dtype=np.float32),
        "ones_row": np.ones((1, 128), dtype=np.float32),
    }


def kernel(**inputs):
    nc = get_program()
    in_maps = [make_inputs(inputs, c) for c in range(8)]
    res = run_bass_kernel_spmd(nc, in_maps, core_ids=list(range(8)))
    y = np.empty((4, S, D), dtype=np.float32)
    for c in range(8):
        b, hh = c // 2, c % 2
        y[b, hh * T:(hh + 1) * T] = res.results[c]["y"]
    return y
